# version 06
# baseline (speedup 1.0000x reference)
import numpy as np
from contextlib import ExitStack

import concourse.bass as bass
import concourse.mybir as mybir
from concourse.bass_utils import run_bass_kernel_spmd

F32 = mybir.dt.float32
BF16 = mybir.dt.bfloat16
U8 = mybir.dt.uint8
AF = mybir.ActivationFunctionType
ALU = mybir.AluOpType
AX = mybir.AxisListType

NCORES = 8
D = 2048
S = 2048
H = 5632
T = 16384
TC = T // NCORES
DC = D // 128
SC = S // 128
HC = H // 128
NJ = 2
TJ = TC // NJ
NTA = TJ // 512
TTA = 512
NTB = 8
TTB = TC // NTB
NSG = 8
SGW = S // NSG
EPS = 1e-5
ZP = 128.0

_NC_CACHE = {}


class _Sem:

    def __init__(self, es, nc, name, step=1):
        self.sem = es.enter_context(nc.semaphore(name))
        self.n = 0
        self.step = step

    def inc(self, inst, k=None):
        k = self.step if k is None else k
        inst.then_inc(self.sem, k)
        self.n += k
        return self.n

    def wait(self, eng, n=None):
        n = self.n if n is None else n
        if n > 0:
            eng.wait_ge(self.sem, n)


def build_program(debug=False):
    nc = bass.Bass(trn_type="TRN2", target_bir_lowering=False)
    es = ExitStack()

    xT = nc.dram_tensor("xT", [D, TC], F32, kind="ExternalInput").ap()
    wpT = [nc.dram_tensor(f"w{i}T", [D, S], F32, kind="ExternalInput").ap()
           for i in range(5)]
    woT = nc.dram_tensor("woT", [S, D], F32, kind="ExternalInput").ap()
    wgT4 = nc.dram_tensor("wgT4", [HC, DC, 128, 128], U8, kind="ExternalInput").ap()
    wuT4 = nc.dram_tensor("wuT4", [HC, DC, 128, 128], U8, kind="ExternalInput").ap()
    wdT4 = nc.dram_tensor("wdT4", [HC, DC, 128, 128], U8, kind="ExternalInput").ap()
    rmsw_i = nc.dram_tensor("rmsw", [128, DC], F32, kind="ExternalInput").ap()
    wfb_i = nc.dram_tensor("wfb", [128, SC], F32, kind="ExternalInput").ap()
    wob_i = nc.dram_tensor("wob", [128, DC], F32, kind="ExternalInput").ap()
    sgv_i = nc.dram_tensor("sgv", [128, HC], F32, kind="ExternalInput").ap()
    suv_i = nc.dram_tensor("suv", [128, HC], F32, kind="ExternalInput").ap()
    sdv_i = nc.dram_tensor("sdv", [128, DC], F32, kind="ExternalInput").ap()
    msk_i = nc.dram_tensor("msk", [128, NCORES], F32, kind="ExternalInput").ap()
    outT = nc.dram_tensor("outT", [D, TC], F32, kind="ExternalOutput").ap()
    spk = "ExternalOutput" if debug else "Internal"
    sp_f = nc.dram_tensor("sp_f", [S, TC], BF16, kind=spk).ap()
    sp_z = nc.dram_tensor("sp_z", [S, TC], BF16, kind=spk).ap()
    sp_og = nc.dram_tensor("sp_og", [S, TC], BF16, kind=spk).ap()
    sp_g = nc.dram_tensor("sp_g", [S, TC], BF16, kind=spk).ap()
    sp_woh = nc.dram_tensor("sp_woh", [DC, 128, S], BF16).ap()
    sp_wol = nc.dram_tensor("sp_wol", [DC, 128, S], BF16).ap()
    cc_in = nc.dram_tensor("cc_in", [2, 128, SC], F32)
    cc_out = nc.dram_tensor("cc_out", [2 * NCORES, 128, SC], F32)
    if debug:
        dbg_x1 = nc.dram_tensor("dbg_x1", [D, TC], F32, kind="ExternalOutput").ap()
        dbg_yr = nc.dram_tensor("dbg_yr", [D, TC], F32, kind="ExternalOutput").ap()
        dbg_y = nc.dram_tensor("dbg_y", [D, TC], F32, kind="ExternalOutput").ap()
        dbg_lp = nc.dram_tensor("dbg_lp", [4, 128, SC], F32, kind="ExternalOutput").ap()

    def sb(name, shape, dt, ctx=None):
        return (ctx or es).enter_context(nc.sbuf_tensor(name, shape, dt))

    ones32 = sb("ones32", [128, 128], F32)
    ones16 = sb("ones16", [128, 128], BF16)
    rmsw = sb("rmsw_s", [128, DC], F32)
    wfb = sb("wfb_s", [128, SC], F32)
    wob = sb("wob_s", [128, DC], F32)
    sgv = sb("sgv_s", [128, HC], F32)
    suv = sb("suv_s", [128, HC], F32)
    sdv = sb("sdv_s", [128, DC], F32)
    mskv = sb("msk_s", [128, NCORES], F32)
    cst = sb("cst", [128, 4], F32)
    LPl = sb("LPl", [128, SC], F32)
    LPp = sb("LPp", [128, SC], F32)
    Lst = sb("Lst", [128, SC], F32)
    Pcol = sb("Pcol", [128, SC], F32)
    ent = sb("ent", [128, SC], F32)
    ptmp = sb("ptmp", [128, SC], F32)
    ptm2 = sb("ptm2", [128, SC], F32)
    hstate = sb("hstate", [128, SC], F32)
    gl = [sb(f"gl{j}", [128, SC], F32) for j in range(NCORES - 1)]
    gp = [sb(f"gp{j}", [128, SC], F32) for j in range(NCORES - 1)]

    fzf = sb("fzf", [128, SC * TTB], BF16)
    fzz = sb("fzz", [128, SC * TTB], BF16)
    fog = sb("fog", [128, SC * TTB], BF16)
    hbuf = sb("hbuf", [128, 8 * TTB], F32)
    pb = [es.enter_context(nc.psum_tensor(f"pb{i}", [128, 512], F32))
          for i in range(8)]
    bank = [_Sem(es, nc, f"bank{i}") for i in range(8)]
    bank_uses = [0] * 8

    qx = _Sem(es, nc, "qx", 16)
    qw = _Sem(es, nc, "qw", 16)
    qsp = _Sem(es, nc, "qsp", 16)
    qcc = _Sem(es, nc, "qcc", 16)
    qfz = _Sem(es, nc, "qfz", 16)
    qgw = _Sem(es, nc, "qgw", 16)
    qgr = _Sem(es, nc, "qgr", 16)
    qwo = _Sem(es, nc, "qwo", 16)
    qxs = _Sem(es, nc, "qxs", 16)
    qfw = _Sem(es, nc, "qfw", 16)
    qout = _Sem(es, nc, "qout", 16)
    qvec = _Sem(es, nc, "qvec", 16)
    s_te = _Sem(es, nc, "s_te")
    s_dve = _Sem(es, nc, "s_dve")
    s_act = _Sem(es, nc, "s_act")
    s_cc = _Sem(es, nc, "s_cc")
    s_grd = _Sem(es, nc, "s_grd")
    s_xs = _Sem(es, nc, "s_xs")

    block = es.enter_context(nc.Block())
    E = {}

    def te_group(tensor, bank_i, mms, width):
        bank[bank_i].wait(tensor, bank_uses[bank_i])
        n = len(mms)
        last = None
        for i, (lh, rh) in enumerate(mms):
            last = tensor.matmul(pb[bank_i][:, 0:width], lh, rh,
                                 start=(i == 0), stop=(i == n - 1))
        return s_te.inc(last)

    def evac_done(inst, bank_i):
        bank_uses[bank_i] += 1
        inst.then_inc(bank[bank_i].sem, 1)
        bank[bank_i].n += 1

    def bank_evt(bank_i):
        return ("bank", bank_i, bank[bank_i].n)

    def wait_evt(eng, evt):
        kind, i, n = evt
        if kind == "bank":
            if n > 0:
                eng.wait_ge(bank[i].sem, n)
        elif kind == "xs":
            if n > 0:
                eng.wait_ge(s_xs.sem, n)
        elif kind == "grd":
            if n > 0:
                eng.wait_ge(s_grd.sem, n)

    @block.sync
    def _(sync):
        for dst, src in ((rmsw, rmsw_i), (wfb, wfb_i), (wob, wob_i),
                         (sgv, sgv_i), (suv, suv_i), (sdv, sdv_i),
                         (mskv, msk_i)):
            sync.dma_start(out=dst[:], in_=src).then_inc(qvec.sem, 16)
    qvec.n = 7 * 16

    E[("grd_scan_pre", 0, 0)] = 1
    @block.vector
    def _(vector):
        vector.memset(ones32[:], 1.0)
        vector.memset(ones16[:], 1.0)
        vector.memset(hstate[:], 0.0)
        vector.memset(ent[:], 0.0)
        vector.memset(Lst[:], 0.0)
        vector.memset(cst[:, 0:1], EPS)
        vector.memset(cst[:, 1:2], 1.0 / D)
        vector.memset(cst[:, 2:3], 2.0)
        vector.memset(cst[:, 3:4], -ZP)
        g = vector.engine_nop()
        s_grd.inc(g)
    E["grd_setup"] = s_grd.n

    ctxA = ExitStack()
    xstrm = [sb(f"xstrm{i}", [128, TJ], F32, ctxA) for i in range(2)]
    sqb = sb("sqb", [128, TJ], F32, ctxA)
    inv_rep = sb("inv_rep", [128, TJ], F32, ctxA)
    xnh = [sb(f"xnh{i}", [128, TJ], BF16, ctxA) for i in range(DC)]
    xnl = [sb(f"xnl{i}", [128, TJ], BF16, ctxA) for i in range(DC)]
    wland = [sb(f"wland{i}", [128, DC * SGW], F32, ctxA) for i in range(1)]
    wh_b = [sb(f"wh{i}", [128, DC * SGW], BF16, ctxA) for i in range(2)]
    wl_b = [sb(f"wl{i}", [128, DC * SGW], BF16, ctxA) for i in range(2)]
    bigA = [sb(f"bigA{c}", [128, TJ], F32, ctxA) for c in range(2)]
    preA = [sb(f"preA{c}", [128, TJ], F32, ctxA) for c in range(2)]
    f_bf = [sb(f"fbf{c}", [128, TJ], BF16, ctxA) for c in range(2)]
    z_bf = [sb(f"zbf{c}", [128, TJ], BF16, ctxA) for c in range(2)]
    og_bf = [sb(f"ogbf{c}", [128, TJ], BF16, ctxA) for c in range(2)]

    wo_split_after = []
    for i in range(SC):
        @block.gpsimd
        def _(gpsimd, i=i):
            if i >= 1:
                s_dve.wait(gpsimd, wo_split_after[i - 1])
            gpsimd.dma_start(out=wland[0][:, 0:D],
                             in_=woT[i * 128:(i + 1) * 128, :]
                             ).then_inc(qwo.sem, 16)
        qwo.n += 16
        E[("qwoland", i)] = qwo.n

        @block.vector
        def _(vector, i=i):
            qwo.wait(vector, E[("qwoland", i)])
            if i >= 2:
                qwo.wait(vector, E[("qwosp", i - 2)])
            vector.tensor_copy(wh_b[i % 2][:, 0:D], wland[0][:, 0:D])
            v = vector.tensor_tensor(wl_b[i % 2][:, 0:D], wland[0][:, 0:D],
                                     wh_b[i % 2][:, 0:D], ALU.subtract)
            s_dve.inc(v)
        wo_split_after.append(s_dve.n)

        @block.gpsimd
        def _(gpsimd, i=i):
            s_dve.wait(gpsimd, wo_split_after[i])
            for dcg in range(4):
                dst = sp_woh[dcg * 4:(dcg + 1) * 4, :, i * 128:(i + 1) * 128]
                dst = dst.rearrange("c p d -> p c d")
                srcv = wh_b[i % 2][:, dcg * 512:(dcg + 1) * 512]
                srcv = srcv.rearrange("p (c d) -> p c d", c=4)
                gpsimd.dma_start(out=dst, in_=srcv).then_inc(qwo.sem, 16)
                dst = sp_wol[dcg * 4:(dcg + 1) * 4, :, i * 128:(i + 1) * 128]
                dst = dst.rearrange("c p d -> p c d")
                srcv = wl_b[i % 2][:, dcg * 512:(dcg + 1) * 512]
                srcv = srcv.rearrange("p (c d) -> p c d", c=4)
                gpsimd.dma_start(out=dst, in_=srcv).then_inc(qwo.sem, 16)
        qwo.n += 8 * 16
        E[("qwosp", i)] = qwo.n
    E["qwo_staged"] = qwo.n

    PROJ = ("zg", "z", "f", "ga", "gb")
    nset = 0
    te_after_set = []
    split_after = []

    for j in range(NJ):
        tj0 = j * TJ

        for i in range(DC):
            @block.sync
            def _(sync, j=j, i=i, tj0=tj0):
                if j > 0 and i < 2:
                    s_dve.wait(sync, E[("xn_i", j - 1, DC - 2 + i)])
                if i >= 2:
                    s_act.wait(sync, E[("sq_i", j, i - 2)])
                sync.dma_start(out=xstrm[i % 2][:],
                               in_=xT[i * 128:(i + 1) * 128, tj0:tj0 + TJ]
                               ).then_inc(qx.sem, 16)
            qx.n += 16
            E[("qx", j, i)] = qx.n

            @block.scalar
            def _(scalar, j=j, i=i):
                if j == 0 and i == 0:
                    qvec.wait(scalar)
                    s_grd.wait(scalar, E["grd_setup"])
                qx.wait(scalar, E[("qx", j, i)])
                if j > 0 and i == 0:
                    s_dve.wait(scalar, E[("dve_xn", j - 1)])
                if i >= 1:
                    s_te.wait(scalar, s_te.n)
                a = scalar.activation(sqb[:], xstrm[i % 2][:], AF.Square)
                s_act.inc(a)
            E[("sq_i", j, i)] = s_act.n

            @block.tensor
            def _(tensor, j=j, i=i):
                if j == 0 and i == 0:
                    s_grd.wait(tensor, E["grd_setup"])
                if i == 0:
                    for tt in range(NTA):
                        bank[tt].wait(tensor, bank_uses[tt])
                s_act.wait(tensor)
                last = None
                for tt in range(NTA):
                    last = tensor.matmul(pb[tt][:], ones32[:],
                                         sqb[:, tt * TTA:(tt + 1) * TTA],
                                         start=(i == 0), stop=(i == DC - 1))
                s_te.inc(last)

        @block.scalar
        def _(scalar, j=j):
            s_te.wait(scalar)
            for tt in range(NTA):
                a = scalar.activation(inv_rep[:, tt * TTA:(tt + 1) * TTA],
                                      pb[tt][:], AF.Sqrt, bias=cst[:, 0:1],
                                      scale=cst[:, 1:2])
                evac_done(a, tt)
        E[("act_inv", j)] = bank_evt(NTA - 1)

        for i in range(DC):
            @block.sync
            def _(sync, j=j, i=i, tj0=tj0):
                if i == 0:
                    wait_evt(sync, E[("act_inv", j)])
                if i >= 2:
                    s_dve.wait(sync, E[("xn_i", j, i - 2)])
                sync.dma_start(out=xstrm[i % 2][:],
                               in_=xT[i * 128:(i + 1) * 128, tj0:tj0 + TJ]
                               ).then_inc(qx.sem, 16)
            qx.n += 16
            E[("qx1", j, i)] = qx.n

            @block.vector
            def _(vector, j=j, i=i):
                if i == 0:
                    wait_evt(vector, E[("act_inv", j)])
                    if j == 0:
                        qvec.wait(vector)
                    vector.reciprocal(sqb[:], inv_rep[:])
                    vector.tensor_copy(inv_rep[:], sqb[:])
                qx.wait(vector, E[("qx1", j, i)])
                vector.scalar_tensor_tensor(sqb[:], xstrm[i % 2][:],
                                            rmsw[:, i:i + 1], inv_rep[:],
                                            ALU.mult, ALU.mult)
                vector.tensor_copy(xnh[i][:], sqb[:])
                v = vector.tensor_tensor(xnl[i][:], sqb[:], xnh[i][:],
                                         ALU.subtract)
                s_dve.inc(v)
            E[("xn_i", j, i)] = s_dve.n
        E[("dve_xn", j)] = s_dve.n

        for sg in range(NSG):
            for pi, kind in enumerate(PROJ):
                idx = nset
                buf = idx % 2
                w_ap = wpT[pi]

                @block.sync
                def _(sync, sg=sg, idx=idx, buf=buf, w_ap=w_ap):
                    if idx < 2:
                        qwo.wait(sync, E["qwo_staged"])
                    if idx >= 1:
                        s_dve.wait(sync, split_after[idx - 1])
                    src = w_ap[:, sg * SGW:(sg + 1) * SGW]
                    src = src.rearrange("(c p) s -> p c s", p=128)
                    dst = wland[0][:].rearrange("p (c s) -> p c s", c=DC)
                    sync.dma_start(out=dst, in_=src).then_inc(qw.sem, 16)
                qw.n += 16

                @block.vector
                def _(vector, idx=idx, buf=buf):
                    qw.wait(vector, (idx + 1) * 16)
                    if idx < 2:
                        qwo.wait(vector, E["qwo_staged"])
                    if idx >= 2:
                        s_te.wait(vector, te_after_set[idx - 2])
                    vector.tensor_copy(wh_b[buf][:], wland[0][:])
                    v = vector.tensor_tensor(wl_b[buf][:], wland[0][:],
                                             wh_b[buf][:], ALU.subtract)
                    s_dve.inc(v)
                split_after.append(s_dve.n)

                @block.tensor
                def _(tensor, j=j, idx=idx, buf=buf):
                    s_dve.wait(tensor, split_after[idx])
                    s_dve.wait(tensor, E[("dve_xn", j)])
                    for c in range(2):
                        for tt in range(NTA):
                            g = ((idx * 2 + c) * NTA + tt) % 8
                            mms = []
                            for kk in range(DC):
                                lh_h = wh_b[buf][:, kk * SGW + c * 128:
                                                 kk * SGW + (c + 1) * 128]
                                lh_l = wl_b[buf][:, kk * SGW + c * 128:
                                                 kk * SGW + (c + 1) * 128]
                                rh_h = xnh[kk][:, tt * TTA:(tt + 1) * TTA]
                                rh_l = xnl[kk][:, tt * TTA:(tt + 1) * TTA]
                                mms += [(lh_h, rh_h), (lh_h, rh_l),
                                        (lh_l, rh_h)]
                            te_group(tensor, g, mms, TTA)
                te_after_set.append(s_te.n)

                @block.scalar
                def _(scalar, j=j, sg=sg, idx=idx, kind=kind):
                    base = te_after_set[idx] - 2 * NTA
                    if kind == "zg" and (j, sg) != (0, 0):
                        qsp.wait(scalar, E[("qsp_need", j, sg)])
                        s_dve.wait(scalar, E[("scan_prev", j, sg)])
                    if kind in ("ga", "gb"):
                        s_dve.wait(scalar, E[("comb", j, sg, "z")])
                    for c in range(2):
                        ci = sg * 2 + c
                        for tt in range(NTA):
                            gi = c * NTA + tt
                            g = ((idx * 2 + c) * NTA + tt) % 8
                            s_te.wait(scalar, base + gi + 1)
                            psl = pb[g][:, 0:TTA]
                            dst = slice(tt * TTA, (tt + 1) * TTA)
                            if kind == "zg":
                                a = scalar.activation(bigA[c][:, dst], psl,
                                                      AF.Sigmoid)
                            elif kind == "z":
                                a = scalar.activation(preA[c][:, dst], psl,
                                                      AF.Copy)
                            elif kind == "f":
                                a = scalar.activation(f_bf[c][:, dst], psl,
                                                      AF.Sigmoid,
                                                      bias=wfb[:, ci:ci + 1])
                            elif kind == "ga":
                                a = scalar.activation(bigA[c][:, dst], psl,
                                                      AF.Silu)
                            else:
                                a = scalar.activation(preA[c][:, dst], psl,
                                                      AF.Copy)
                            evac_done(a, g)
                last_g = ((idx * 2 + 1) * NTA + NTA - 1) % 8
                E[("evac", j, sg, kind)] = bank_evt(last_g)

                if kind in ("z", "gb"):
                    @block.vector
                    def _(vector, j=j, sg=sg, kind=kind):
                        wait_evt(vector, E[("evac", j, sg, kind)])
                        if (j, sg) != (0, 0):
                            qsp.wait(vector, E[("qsp_need", j, sg)])
                        for c in range(2):
                            dst = z_bf[c] if kind == "z" else og_bf[c]
                            v = vector.tensor_tensor(dst[:], preA[c][:],
                                                     bigA[c][:], ALU.mult)
                        s_dve.inc(v)
                    E[("comb", j, sg, kind)] = s_dve.n
                nset += 1

            @block.vector
            def _(vector, j=j, sg=sg):
                wait_evt(vector, E[("evac", j, sg, "f")])
                s_grd.wait(vector, E[("grd_scan_pre", j, sg)])

                def grd(inst=None):
                    g = vector.engine_nop()
                    s_grd.inc(g)
                    vector.wait_ge(s_grd.sem, s_grd.n)

                for c in range(2):
                    ci = sg * 2 + c
                    vector.tensor_tensor_scan(hbuf[:, 0:TJ], f_bf[c][:], z_bf[c][:],
                                              Lst[:, ci:ci + 1],
                                              ALU.mult, ALU.add)
                    grd()
                    if j == 0:
                        vector.tensor_copy(Lst[:, ci:ci + 1],
                                           hbuf[:, TJ - 1:TJ])
                    else:
                        vector.tensor_copy(LPl[:, ci:ci + 1],
                                           hbuf[:, TJ - 1:TJ])
                    grd()
                    vector.tensor_reduce(Pcol[:, ci:ci + 1], f_bf[c][:],
                                         AX.X, ALU.mult)
                    grd()
                    if j == 0:
                        v = vector.tensor_copy(LPp[:, ci:ci + 1],
                                               Pcol[:, ci:ci + 1])
                    else:
                        v = vector.tensor_tensor(LPp[:, ci:ci + 1],
                                                 LPp[:, ci:ci + 1],
                                                 Pcol[:, ci:ci + 1], ALU.mult)
                    grd()
                s_dve.inc(v)
            E[("scan", j, sg)] = s_dve.n
            E[("grd_scan_pre", j, sg + 1)] = s_grd.n
            if sg == NSG - 1 and j == 0:
                E[("grd_scan_pre", 1, 0)] = s_grd.n
            E[("scan_prev", j, sg + 1)] = s_dve.n
            if sg == NSG - 1 and j == 0:
                E[("scan_prev", 1, 0)] = s_dve.n

            @block.gpsimd
            def _(gpsimd, j=j, sg=sg, tj0=tj0):
                s_dve.wait(gpsimd, E[("scan", j, sg)])
                for c in range(2):
                    ci = sg * 2 + c
                    rows = slice(ci * 128, (ci + 1) * 128)
                    tcols = slice(tj0, tj0 + TJ)
                    gpsimd.dma_start(out=sp_f[rows, tcols], in_=f_bf[c][:]
                                     ).then_inc(qsp.sem, 16)
                    gpsimd.dma_start(out=sp_z[rows, tcols], in_=z_bf[c][:]
                                     ).then_inc(qsp.sem, 16)
                    gpsimd.dma_start(out=sp_og[rows, tcols], in_=og_bf[c][:]
                                     ).then_inc(qsp.sem, 16)
            qsp.n += 6 * 16
            E[("qsp_need", j, sg + 1)] = qsp.n
            if sg == NSG - 1 and j == 0:
                E[("qsp_need", 1, 0)] = qsp.n

        @block.vector
        def _(vector, j=j):
            g = vector.engine_nop()
            s_grd.inc(g)

    @block.gpsimd
    def _(gpsimd):
        qsp.wait(gpsimd, E[("qsp_need", 1, 0)])
        for (srct, dstt) in ((sp_f, fzf), (sp_z, fzz), (sp_og, fog)):
            for q in range(4):
                rows = srct[q * 512:(q + 1) * 512, 0:TTB]
                rows = rows.rearrange("(c p) t -> p c t", p=128)
                dv = dstt[:, q * 4 * TTB:(q + 1) * 4 * TTB]
                dv = dv.rearrange("p (c t) -> p c t", c=4)
                gpsimd.dma_start(out=dv, in_=rows).then_inc(qfz.sem, 16)
    qfz.n += 12 * 16
    E[("qfz", 0)] = qfz.n

    @block.gpsimd
    def _(gpsimd):
        s_dve.wait(gpsimd, E[("scan", NJ - 1, NSG - 1)])
        gpsimd.dma_start(
            out=cc_in.ap()[0:1].rearrange("o p c -> p o c"),
            in_=LPl[:].unsqueeze(1)).then_inc(qcc.sem, 16)
        gpsimd.dma_start(
            out=cc_in.ap()[1:2].rearrange("o p c -> p o c"),
            in_=LPp[:].unsqueeze(1)).then_inc(qcc.sem, 16)
        gpsimd.wait_ge(qcc.sem, 32)
        cc = gpsimd.collective_compute(
            "AllGather", ALU.bypass, replica_groups=[list(range(NCORES))],
            ins=[cc_in.ap().opt()], outs=[cc_out.ap().opt()])
        s_cc.inc(cc)
        s_cc.wait(gpsimd)
        for jj in range(NCORES - 1):
            gpsimd.dma_start(
                out=gl[jj][:].unsqueeze(1),
                in_=cc_out.ap()[2 * jj:2 * jj + 1].rearrange(
                    "o p c -> p o c")).then_inc(qcc.sem, 16)
            gpsimd.dma_start(
                out=gp[jj][:].unsqueeze(1),
                in_=cc_out.ap()[2 * jj + 1:2 * jj + 2].rearrange(
                    "o p c -> p o c")).then_inc(qcc.sem, 16)
    qcc.n = 32 + (NCORES - 1) * 32

    @block.vector
    def _(vector):
        qcc.wait(vector)

        def grd():
            g = vector.engine_nop()
            s_grd.inc(g)
            vector.wait_ge(s_grd.sem, s_grd.n)

        for jj in range(NCORES - 1):
            vector.tensor_tensor(ptmp[:], gp[jj][:], ent[:], ALU.mult)
            grd()
            vector.tensor_tensor(ptmp[:], ptmp[:], gl[jj][:], ALU.add)
            grd()
            vector.tensor_tensor(ptm2[:], ptmp[:], ent[:], ALU.subtract)
            grd()
            vector.scalar_tensor_tensor(ent[:], ptm2[:], mskv[:, jj:jj + 1],
                                        ent[:], ALU.mult, ALU.add)
            grd()
        vector.tensor_copy(hstate[:], ent[:])
        g = vector.engine_nop()
        s_grd.inc(g)
    E["grd_entry"] = s_grd.n

    te_A, dve_A, act_A = s_te.n, s_dve.n, s_act.n
    ctxA.close()

    ctxB = ExitStack()


    wob_h = [sb(f"wobh{i}", [128, S], BF16, ctxB) for i in range(2)]
    wob_l = [sb(f"wobl{i}", [128, S], BF16, ctxB) for i in range(2)]
    gated = [sb(f"gated{i}", [128, SC * TTB], BF16, ctxB) for i in range(2)]
    xsb = sb("xsb", [128, DC * TTB], F32, ctxB)
    big_a = [sb(f"biga{i}", [128, TTB], F32, ctxB) for i in range(DC)]
    vb = [sb(f"vb{i}", [128, TTB], F32, ctxB) for i in range(DC)]
    x1f = [sb(f"x1f{i}", [128, TTB], F32, ctxB) for i in range(DC)]
    x1h = [sb(f"x1h{i}", [128, TTB], BF16, ctxB) for i in range(DC)]
    x1l = [sb(f"x1l{i}", [128, TTB], BF16, ctxB) for i in range(DC)]
    acth = [sb(f"acth{i}", [128, TTB], BF16, ctxB) for i in range(22)]
    actl = [sb(f"actl{i}", [128, TTB], BF16, ctxB) for i in range(22)]
    sqt = sb("sqt", [128, TTB], F32, ctxB)
    sqt2 = sb("sqt2", [128, TTB], F32, ctxB)
    mu_r = sb("mu_r", [128, TTB], F32, ctxB)
    m2_r = sb("m2_r", [128, TTB], F32, ctxB)
    iv_r = sb("iv_r", [128, TTB], F32, ctxB)
    csx = sb("csx", [128, TTB], F32, ctxB)
    csa = sb("csa", [128, TTB], F32, ctxB)
    gt_f = sb("gt_f", [128, TTB], F32, ctxB)
    up_f = sb("up_f", [128, TTB], F32, ctxB)
    af_f = sb("af_f", [128, TTB], F32, ctxB)
    ge_f = sb("ge_f", [128, TTB], F32, ctxB)
    ue_f = sb("ue_f", [128, TTB], F32, ctxB)
    wgt = [sb(f"wgt{i}", [128, DC * 128], BF16, ctxB) for i in range(2)]
    wut = [sb(f"wut{i}", [128, DC * 128], BF16, ctxB) for i in range(2)]
    wdt = [sb(f"wdt{i}", [128, 8 * 128], BF16, ctxB) for i in range(2)]
    outb = [sb(f"outb{i}", [128, TTB], F32, ctxB) for i in range(3)]

    def emit_bpre(k):
        t0 = k * TTB
        gb_ = gated[k % 2]
        if k > 0:
            @block.gpsimd
            def _(gpsimd, k=k, t0=t0):
                if k >= 4:
                    qsp.wait(gpsimd)
                s_grd.wait(gpsimd, E[("bscan_grd", k - 1)])
                for (srct, dstt) in ((sp_f, fzf), (sp_z, fzz), (sp_og, fog)):
                    for q in range(4):
                        rows = srct[q * 512:(q + 1) * 512, t0:t0 + TTB]
                        rows = rows.rearrange("(c p) t -> p c t", p=128)
                        dv = dstt[:, q * 4 * TTB:(q + 1) * 4 * TTB]
                        dv = dv.rearrange("p (c t) -> p c t", c=4)
                        gpsimd.dma_start(out=dv, in_=rows).then_inc(qfz.sem, 16)
            qfz.n += 12 * 16
            E[("qfz", k)] = qfz.n

        @block.vector
        def _(vector, k=k, gb_=gb_):
            qfz.wait(vector, E[("qfz", k)])
            s_grd.wait(vector, E["grd_entry"] if k == 0
                       else E[("bscan_grd", k - 1)])
            if k >= 2:
                s_te.wait(vector, E[("wout_te", k - 2)])
            for half8 in range(2):
                for cc8 in range(8):
                    c = half8 * 8 + cc8
                    cs = slice(c * TTB, (c + 1) * TTB)
                    hs = slice(cc8 * TTB, (cc8 + 1) * TTB)
                    vector.tensor_tensor_scan(hbuf[:, hs], fzf[:, cs],
                                              fzz[:, cs], hstate[:, c:c + 1],
                                              ALU.mult, ALU.add)
                    vector.tensor_tensor(gb_[:, cs], fog[:, cs], hbuf[:, hs],
                                         ALU.mult)
                g = vector.engine_nop()
                s_grd.inc(g)
                vector.wait_ge(s_grd.sem, s_grd.n)
                vector.tensor_copy(hstate[:, half8 * 8:half8 * 8 + 8],
                                   hbuf[:, TTB - 1:8 * TTB:TTB])
                g = vector.engine_nop()
                s_grd.inc(g)
                vector.wait_ge(s_grd.sem, s_grd.n)
        E[("bscan_grd", k)] = s_grd.n

        if debug:
            @block.gpsimd
            def _(gpsimd, k=k, t0=t0, gb_=gb_):
                s_grd.wait(gpsimd, E[("bscan_grd", k)])
                for q in range(4):
                    rows = sp_g[q * 512:(q + 1) * 512, t0:t0 + TTB]
                    rows = rows.rearrange("(c p) t -> p c t", p=128)
                    sv = gb_[:, q * 4 * TTB:(q + 1) * 4 * TTB]
                    sv = sv.rearrange("p (c t) -> p c t", c=4)
                    gpsimd.dma_start(out=rows, in_=sv).then_inc(qgw.sem, 16)
            qgw.n += 64


    def emit_wout_dc(k, dc):
        gb_ = gated[k % 2]
        t0 = k * TTB

        @block.sync
        def _(sync, k=k, dc=dc):
            seq = 16 * k + dc
            if k == 0 and dc == 0:
                qwo.wait(sync, E["qwo_staged"])
                s_te.wait(sync, te_A)
                s_dve.wait(sync, dve_A)
                s_act.wait(sync, act_A)
                qsp.wait(sync)
            if seq >= 2:
                s_te.wait(sync, E.get(("wout_dc", (seq - 2) // 16,
                                       (seq - 2) % 16), 0))
            sync.dma_start(out=wob_h[dc % 2][:], in_=sp_woh[dc]
                           ).then_inc(qwo.sem, 16)
            sync.dma_start(out=wob_l[dc % 2][:], in_=sp_wol[dc]
                           ).then_inc(qwo.sem, 16)
        qwo.n += 32
        E[("qwob", k, dc)] = qwo.n

        @block.tensor
        def _(tensor, k=k, dc=dc, gb_=gb_):
            if dc == 0:
                s_grd.wait(tensor, E[("bscan_grd", k)])
            qwo.wait(tensor, E[("qwob", k, dc)])
            mms = []
            for c in range(SC):
                cs = slice(c * TTB, (c + 1) * TTB)
                mms += [(wob_h[dc % 2][:, c * 128:(c + 1) * 128], gb_[:, cs]),
                        (wob_l[dc % 2][:, c * 128:(c + 1) * 128], gb_[:, cs])]
            te_group(tensor, dc % 8, mms, TTB)
        E[("wout_dc", k, dc)] = s_te.n

        @block.scalar
        def _(scalar, k=k, dc=dc):
            if dc in (0, 6) and k >= 1:
                s_dve.wait(scalar, E[("v2", k - 1)])
                if debug:
                    qcc.wait(scalar)
            s_te.wait(scalar, E[("wout_dc", k, dc)])
            a = scalar.activation(big_a[dc][:], pb[dc % 8][:, 0:TTB],
                                  AF.Identity, bias=wob[:, dc:dc + 1])
            evac_done(a, dc % 8)
            E[("yev_dc", k, dc)] = bank_evt(dc % 8)

    emit_bpre(0)

    for k in range(NTB):
        t0 = k * TTB
        gb_ = gated[k % 2]

        pass
        for dc in (range(DC) if k == 0 else range(6, DC)):
            emit_wout_dc(k, dc)
        E[("wout_te", k)] = s_te.n

        @block.sync
        def _(sync, k=k, t0=t0):
            if k >= 1:
                s_xs.wait(sync, 16 * k)
            for q in range(4):
                rows = xT[q * 512:(q + 1) * 512, t0:t0 + TTB]
                rows = rows.rearrange("(c p) t -> p c t", p=128)
                dv = xsb[:, q * 4 * TTB:(q + 1) * 4 * TTB]
                dv = dv.rearrange("p (c t) -> p c t", c=4)
                sync.dma_start(out=dv, in_=rows).then_inc(qxs.sem, 16)
            qxs.n += 64
        E[("qxs", k)] = qxs.n

        @block.vector
        def _(vector, k=k):
            for dc in range(DC):
                wait_evt(vector, E[("yev_dc", k, dc)])
                if dc == 0:
                    qxs.wait(vector, E[("qxs", k)])
                v = vector.scalar_tensor_tensor(
                    vb[dc][:], xsb[:, dc * TTB:(dc + 1) * TTB],
                    cst[:, 2:3], big_a[dc][:], ALU.mult, ALU.add)
                s_xs.inc(v)
        E[("v", k)] = ("xs", 0, s_xs.n)

        def emit_ln(tag, k, src, base_key, per_dc):
            sq_b = (sqt, sqt2)
            for dc in range(DC):
                @block.scalar
                def _(scalar, dc=dc):
                    if per_dc:
                        kind0, _, n0 = E[(base_key, k)]
                        assert kind0 == "xs"
                        scalar.wait_ge(s_xs.sem, n0 - DC + dc + 1)
                    else:
                        s_dve.wait(scalar, E[(base_key, k)])
                    if dc >= 2:
                        s_te.wait(scalar, s_te.n - 1)
                    a = scalar.activation(sq_b[dc % 2][:], src[dc][:], AF.Square)
                    s_act.inc(a)

                @block.tensor
                def _(tensor, dc=dc):
                    s_act.wait(tensor)
                    if dc == 0:
                        bank[6].wait(tensor, bank_uses[6])
                        bank[7].wait(tensor, bank_uses[7])
                    tensor.matmul(pb[6][:, 0:TTB], ones32[:], src[dc][:],
                                  start=(dc == 0), stop=(dc == DC - 1))
                    last = tensor.matmul(pb[7][:, 0:TTB], ones32[:],
                                         sq_b[dc % 2][:],
                                         start=(dc == 0), stop=(dc == DC - 1))
                    s_te.inc(last)

            @block.scalar
            def _(scalar):
                s_te.wait(scalar)
                if tag == "ln1" and k >= 1:
                    s_dve.wait(scalar, E[("out", k - 1, DC - 1)])
                a = scalar.activation(mu_r[:], pb[6][:, 0:TTB], AF.Identity,
                                      scale=cst[:, 1:2])
                evac_done(a, 6)
                a = scalar.activation(m2_r[:], pb[7][:, 0:TTB], AF.Identity,
                                      scale=cst[:, 1:2])
                evac_done(a, 7)
            E[(tag + "_ms", k)] = bank_evt(7)

            @block.vector
            def _(vector):
                wait_evt(vector, E[(tag + "_ms", k)])
                vector.tensor_tensor(sqt[:], mu_r[:], mu_r[:], ALU.mult)
                v = vector.tensor_tensor(m2_r[:], m2_r[:], sqt[:], ALU.subtract)
                s_dve.inc(v)
            E[(tag + "_var", k)] = s_dve.n

            @block.scalar
            def _(scalar):
                s_dve.wait(scalar, E[(tag + "_var", k)])
                a = scalar.activation(iv_r[:], m2_r[:], AF.Sqrt,
                                      bias=cst[:, 0:1])
                s_act.inc(a)
            E[(tag + "_sqrt", k)] = s_act.n

            @block.vector
            def _(vector):
                s_act.wait(vector, E[(tag + "_sqrt", k)])
                vector.reciprocal(m2_r[:], iv_r[:])
                v = vector.tensor_copy(iv_r[:], m2_r[:])
                s_dve.inc(v)
            E[(tag + "_inv", k)] = s_dve.n

        emit_ln("ln1", k, vb, "v", per_dc=True)

        @block.vector
        def _(vector, k=k):
            s_dve.wait(vector, E[("ln1_inv", k)])
            for dc in range(DC):
                vector.tensor_tensor(x1f[dc][:], vb[dc][:], mu_r[:],
                                     ALU.subtract)
                vector.tensor_tensor(x1f[dc][:], x1f[dc][:], iv_r[:], ALU.mult)
                vector.tensor_copy(x1h[dc][:], x1f[dc][:])
                v = vector.tensor_tensor(x1l[dc][:], x1f[dc][:], x1h[dc][:],
                                         ALU.subtract)
            s_dve.inc(v)
        E[("x1", k)] = s_dve.n

        if k + 1 < NTB:
            emit_bpre(k + 1)

        if debug:
            @block.gpsimd
            def _(gpsimd, k=k):
                s_dve.wait(gpsimd, E[("x1", k)])
                for dc in range(DC):
                    gpsimd.dma_start(
                        out=dbg_x1[dc * 128:(dc + 1) * 128,
                                   k * TTB:(k + 1) * TTB],
                        in_=x1f[dc][:]).then_inc(qcc.sem, 16)
                    gpsimd.dma_start(
                        out=dbg_y[dc * 128:(dc + 1) * 128,
                                  k * TTB:(k + 1) * TTB],
                        in_=vb[dc][:]).then_inc(qcc.sem, 16)
                    gpsimd.dma_start(
                        out=dbg_yr[dc * 128:(dc + 1) * 128,
                                   k * TTB:(k + 1) * TTB],
                        in_=big_a[dc][:]).then_inc(qcc.sem, 16)
            qcc.n += 48 * DC

        @block.tensor
        def _(tensor, k=k):
            s_dve.wait(tensor, E[("x1", k)])
            mms = []
            for dc in range(DC):
                mms += [(ones16[:], x1h[dc][:]), (ones16[:], x1l[dc][:])]
            te_group(tensor, 2, mms, TTB)
        E[("csx_te", k)] = s_te.n

        @block.scalar
        def _(scalar, k=k):
            s_te.wait(scalar, E[("csx_te", k)])
            a = scalar.activation(csx[:], pb[2][:, 0:TTB], AF.Copy)
            evac_done(a, 2)
        E[("csx", k)] = bank_evt(2)

        for half in range(2):
            hlo = half * 22
            for hh in range(22):
                hc = hlo + hh

                @block.gpsimd
                def _(gpsimd, k=k, hc=hc):
                    buf = hc % 2
                    if ("gu_te", k, hc - 2) in E:
                        s_te.wait(gpsimd, E[("gu_te", k, hc - 2)])
                    elif k > 0 and hc < 2:
                        s_te.wait(gpsimd, E[("gu_te", k - 1, 42 + hc)])
                    srct = wgT4[hc:hc + 1].rearrange("o c p h -> p (o c) h")
                    dv = wgt[buf][:].rearrange("p (c h) -> p c h", c=DC)
                    gpsimd.dma_start(out=dv, in_=srct).then_inc(qfw.sem, 16)
                    srct = wuT4[hc:hc + 1].rearrange("o c p h -> p (o c) h")
                    dv = wut[buf][:].rearrange("p (c h) -> p c h", c=DC)
                    gpsimd.dma_start(out=dv, in_=srct).then_inc(qfw.sem, 16)
                qfw.n += 32
                E[("gu_w", k, hc)] = qfw.n

                @block.tensor
                def _(tensor, k=k, hc=hc, hh=hh, half=half):
                    buf = hc % 2
                    qfw.wait(tensor, E[("gu_w", k, hc)])
                    if hh == 0 and half == 0:
                        wait_evt(tensor, E[("csx", k)])
                    mms_g, mms_u = [], []
                    for dc in range(DC):
                        wsl = slice(dc * 128, (dc + 1) * 128)
                        mms_g += [(wgt[buf][:, wsl], x1h[dc][:]),
                                  (wgt[buf][:, wsl], x1l[dc][:])]
                        mms_u += [(wut[buf][:, wsl], x1h[dc][:]),
                                  (wut[buf][:, wsl], x1l[dc][:])]
                    gbk, ubk = (3, 4) if hc % 2 == 0 else (5, 6)
                    te_group(tensor, gbk, mms_g, TTB)
                    te_group(tensor, ubk, mms_u, TTB)
                E[("gu_te", k, hc)] = s_te.n

                @block.vector
                def _(vector, k=k, hc=hc):
                    s_te.wait(vector, E[("gu_te", k, hc)])
                    if ("gu_act", k, hc - 1) in E:
                        s_act.wait(vector, E[("gu_act", k, hc - 1)])
                    gbk, ubk = (3, 4) if hc % 2 == 0 else (5, 6)
                    v = vector.scalar_tensor_tensor(ge_f[:], csx[:],
                                                    cst[:, 3:4],
                                                    pb[gbk][:, 0:TTB],
                                                    ALU.mult, ALU.add)
                    evac_done(v, gbk)
                    v = vector.scalar_tensor_tensor(ue_f[:], csx[:],
                                                    cst[:, 3:4],
                                                    pb[ubk][:, 0:TTB],
                                                    ALU.mult, ALU.add)
                    evac_done(v, ubk)
                E[("gu_ev", k, hc)] = bank_evt(4 if hc % 2 == 0 else 6)

                @block.scalar
                def _(scalar, k=k, hc=hc):
                    wait_evt(scalar, E[("gu_ev", k, hc)])
                    scalar.activation(gt_f[:], ge_f[:], AF.Silu,
                                      scale=sgv[:, hc:hc + 1])
                    a = scalar.activation(up_f[:], ue_f[:], AF.Identity,
                                          scale=suv[:, hc:hc + 1])
                    s_act.inc(a)
                E[("gu_act", k, hc)] = s_act.n

                @block.vector
                def _(vector, k=k, hc=hc, hh=hh):
                    s_act.wait(vector, E[("gu_act", k, hc)])
                    vector.tensor_tensor(af_f[:], gt_f[:], up_f[:], ALU.mult)
                    vector.tensor_copy(acth[hh][:], af_f[:])
                    v = vector.tensor_tensor(actl[hh][:], af_f[:], acth[hh][:],
                                             ALU.subtract)
                    s_dve.inc(v)
                E[("act", k, hc)] = s_dve.n

            @block.tensor
            def _(tensor, k=k, hlo=hlo):
                s_dve.wait(tensor, E[("act", k, hlo + 21)])
                mms = []
                for hh in range(22):
                    mms += [(ones16[:], acth[hh][:]), (ones16[:], actl[hh][:])]
                te_group(tensor, 2, mms, TTB)
            E[("csa_te", k, half)] = s_te.n

            @block.scalar
            def _(scalar, k=k, half=half):
                s_te.wait(scalar, E[("csa_te", k, half)])
                if half == 1:
                    wait_evt(scalar, E[("dn_ev", k, 0, 1)])
                a = scalar.activation(csa[:], pb[2][:, 0:TTB], AF.Copy)
                evac_done(a, 2)
            E[("csa", k, half)] = bank_evt(2)

            for dh in range(2):
                for hh in range(22):
                    hc = hlo + hh

                    @block.gpsimd
                    def _(gpsimd, k=k, half=half, dh=dh, hh=hh, hc=hc):
                        buf = hh % 2
                        if ("dn_te", k, half, dh, hh - 2) in E:
                            s_te.wait(gpsimd, E[("dn_te", k, half, dh, hh - 2)])
                        elif hh < 2:
                            for key in (("dn_te", k, half, dh - 1, 20 + hh),
                                        ("dn_te", k, half - 1, 1, 20 + hh),
                                        ("dn_te", k - 1, 1, 1, 20 + hh)):
                                if key in E:
                                    s_te.wait(gpsimd, E[key])
                                    break
                        srct = wdT4[hc:hc + 1, dh * 8:(dh + 1) * 8]
                        srct = srct.rearrange("o c p d -> p (o c) d")
                        dv = wdt[buf][:].rearrange("p (c d) -> p c d", c=8)
                        gpsimd.dma_start(out=dv, in_=srct).then_inc(qfw.sem, 16)
                    qfw.n += 16
                    E[("dn_w", k, half, dh, hh)] = qfw.n

                    @block.tensor
                    def _(tensor, k=k, half=half, dh=dh, hh=hh):
                        buf = hh % 2
                        qfw.wait(tensor, E[("dn_w", k, half, dh, hh)])
                        if hh == 0:
                            for b8 in range(8):
                                bank[b8].wait(tensor, bank_uses[b8])
                        last = None
                        for dc8 in range(8):
                            wsl = slice(dc8 * 128, (dc8 + 1) * 128)
                            last = tensor.matmul(pb[dc8][:, 0:TTB],
                                                 wdt[buf][:, wsl], acth[hh][:],
                                                 start=(hh == 0), stop=False)
                            last = tensor.matmul(pb[dc8][:, 0:TTB],
                                                 wdt[buf][:, wsl], actl[hh][:],
                                                 start=False, stop=(hh == 21))
                        s_te.inc(last)
                    E[("dn_te", k, half, dh, hh)] = s_te.n

                @block.vector
                def _(vector, k=k, half=half, dh=dh):
                    s_te.wait(vector, E[("dn_te", k, half, dh, 21)])
                    if debug:
                        qcc.wait(vector)
                    wait_evt(vector, E[("csa", k, half)])
                    for dc8 in range(8):
                        dc = dh * 8 + dc8
                        vector.scalar_tensor_tensor(sqt[:], csa[:],
                                                    cst[:, 3:4],
                                                    pb[dc8][:, 0:TTB],
                                                    ALU.mult, ALU.add)
                        if half == 0:
                            v = vector.tensor_copy(big_a[dc][:], sqt[:])
                        else:
                            v = vector.tensor_tensor(big_a[dc][:], big_a[dc][:],
                                                     sqt[:], ALU.add)
                        evac_done(v, dc8)
                E[("dn_ev", k, half, dh)] = bank_evt(7)

        @block.vector
        def _(vector, k=k):
            if debug:
                qcc.wait(vector)
            v = None
            for dc in range(DC):
                v = vector.scalar_tensor_tensor(vb[dc][:], big_a[dc][:],
                                                sdv[:, dc:dc + 1], x1f[dc][:],
                                                ALU.mult, ALU.add)
            s_dve.inc(v)
        E[("v2", k)] = s_dve.n

        if k + 1 < NTB:
            for dc in range(6):
                emit_wout_dc(k + 1, dc)

        emit_ln("ln2", k, vb, "v2", per_dc=False)

        for dc in range(DC):
            @block.vector
            def _(vector, k=k, dc=dc):
                if dc == 0:
                    s_dve.wait(vector, E[("ln2_inv", k)])
                seq = 16 * k + dc
                if seq >= 3:
                    qout.wait(vector, E[("outdma", (seq - 3) // 16,
                                         (seq - 3) % 16)])
                ob = outb[dc % 3]
                vector.tensor_tensor(ob[:], vb[dc][:], mu_r[:], ALU.subtract)
                v = vector.tensor_tensor(ob[:], ob[:], iv_r[:], ALU.mult)
                s_dve.inc(v)
            E[("out", k, dc)] = s_dve.n

            @block.scalar
            def _(scalar, k=k, dc=dc):
                s_dve.wait(scalar, E[("out", k, dc)])
                scalar.dma_start(out=outT[dc * 128:(dc + 1) * 128,
                                          k * TTB:(k + 1) * TTB],
                                 in_=outb[dc % 3][:]).then_inc(qout.sem, 16)
            qout.n += 16
            E[("outdma", k, dc)] = qout.n

    if debug:
        @block.gpsimd
        def _(gpsimd):
            qout.wait(gpsimd)
            gpsimd.dma_start(out=dbg_lp[0], in_=LPl[:]).then_inc(qcc.sem, 16)
            gpsimd.dma_start(out=dbg_lp[1], in_=LPp[:]).then_inc(qcc.sem, 16)
            gpsimd.dma_start(out=dbg_lp[2], in_=ent[:]).then_inc(qcc.sem, 16)
            gpsimd.dma_start(out=dbg_lp[3], in_=hstate[:]).then_inc(qcc.sem, 16)
            qcc.n += 4 * 16
            qcc.wait(gpsimd)

    @block.scalar
    def _(scalar):
        qout.wait(scalar)

    @block.gpsimd
    def _(gpsimd):
        qout.wait(gpsimd)

    ctxB.close()
    es.close()
    return nc


def _prep_inputs(L0_attn_x, L0_attn_rms_w, L0_attn_Wf_w, L0_attn_Wf_b,
                 L0_attn_Wzg_w, L0_attn_Wz_w, L0_attn_Wga_w, L0_attn_Wgb_w,
                 L0_attn_Wout_w, L0_attn_Wout_b, L0_ffn_wq_gate, L0_ffn_s_gate,
                 L0_ffn_wq_up, L0_ffn_s_up, L0_ffn_wq_down, L0_ffn_s_down):
    f32 = np.float32
    x = np.asarray(L0_attn_x, dtype=f32)[0]
    xT = np.ascontiguousarray(x.T)
    w_order = (L0_attn_Wzg_w, L0_attn_Wz_w, L0_attn_Wf_w, L0_attn_Wga_w,
               L0_attn_Wgb_w)
    wT = [np.ascontiguousarray(np.asarray(w, f32).T) for w in w_order]
    woT = np.ascontiguousarray(np.asarray(L0_attn_Wout_w, f32).T)

    def tile4_hd(wq):
        a = np.asarray(wq).astype(np.uint8)
        aT = a.T
        return np.ascontiguousarray(
            aT.reshape(DC, 128, HC, 128).transpose(2, 0, 1, 3))

    def tile4_dh(wq):
        a = np.asarray(wq).astype(np.uint8)
        aT = a.T
        return np.ascontiguousarray(
            aT.reshape(HC, 128, DC, 128).transpose(0, 2, 1, 3))

    def cols(v, n):
        return np.ascontiguousarray(np.asarray(v, f32).reshape(n, 128).T)

    base = {
        "rmsw": cols(L0_attn_rms_w, DC), "wfb": cols(L0_attn_Wf_b, SC),
        "wob": cols(L0_attn_Wout_b, DC), "sgv": cols(L0_ffn_s_gate, HC),
        "suv": cols(L0_ffn_s_up, HC), "sdv": cols(L0_ffn_s_down, DC),
        "woT": woT, "wgT4": tile4_hd(L0_ffn_wq_gate),
        "wuT4": tile4_hd(L0_ffn_wq_up), "wdT4": tile4_dh(L0_ffn_wq_down),
    }
    for i in range(5):
        base[f"w{i}T"] = wT[i]

    in_maps = []
    for c in range(NCORES):
        m = dict(base)
        m["xT"] = np.ascontiguousarray(xT[:, c * TC:(c + 1) * TC])
        msk = np.zeros((128, NCORES), f32)
        msk[:, :c] = 1.0
        m["msk"] = msk
        in_maps.append(m)
    return in_maps


def kernel(**inputs):
    in_maps = _prep_inputs(**inputs)
    if "prog" not in _NC_CACHE:
        _NC_CACHE["prog"] = build_program()
    nc = _NC_CACHE["prog"]
    res = run_bass_kernel_spmd(nc, in_maps, core_ids=list(range(NCORES)))
    out = np.empty((1, T, D), dtype=np.float32)
    for c in range(NCORES):
        out[0, c * TC:(c + 1) * TC, :] = res.results[c]["outT"].T
    return out


# version 07 (final)
# speedup vs baseline: 1.1554x; 1.1554x over previous
import numpy as np
from contextlib import ExitStack

import concourse.bass as bass
import concourse.mybir as mybir
from concourse.bass_utils import run_bass_kernel_spmd

F32 = mybir.dt.float32
BF16 = mybir.dt.bfloat16
U8 = mybir.dt.uint8
AF = mybir.ActivationFunctionType
ALU = mybir.AluOpType
AX = mybir.AxisListType

NCORES = 8
D = 2048
S = 2048
H = 5632
T = 16384
TC = T // NCORES
DC = D // 128
SC = S // 128
HC = H // 128
NJ = 2
TJ = TC // NJ
NTA = TJ // 512
TTA = 512
NTB = 8
TTB = TC // NTB
NSG = 8
SGW = S // NSG
EPS = 1e-5
ZP = 128.0

_NC_CACHE = {}


class _Sem:

    def __init__(self, es, nc, name, step=1):
        self.sem = es.enter_context(nc.semaphore(name))
        self.n = 0
        self.step = step

    def inc(self, inst, k=None):
        k = self.step if k is None else k
        inst.then_inc(self.sem, k)
        self.n += k
        return self.n

    def wait(self, eng, n=None):
        n = self.n if n is None else n
        if n > 0:
            eng.wait_ge(self.sem, n)


def build_program(debug=False):
    nc = bass.Bass(trn_type="TRN2", target_bir_lowering=False)
    es = ExitStack()

    xT = nc.dram_tensor("xT", [D, TC], F32, kind="ExternalInput").ap()
    wpT = [nc.dram_tensor(f"w{i}T", [D, S], F32, kind="ExternalInput").ap()
           for i in range(5)]
    woT = nc.dram_tensor("woT", [S, D], F32, kind="ExternalInput").ap()
    wgT4 = nc.dram_tensor("wgT4", [HC, DC, 128, 128], U8, kind="ExternalInput").ap()
    wuT4 = nc.dram_tensor("wuT4", [HC, DC, 128, 128], U8, kind="ExternalInput").ap()
    wdT4 = nc.dram_tensor("wdT4", [HC, DC, 128, 128], U8, kind="ExternalInput").ap()
    rmsw_i = nc.dram_tensor("rmsw", [128, DC], F32, kind="ExternalInput").ap()
    wfb_i = nc.dram_tensor("wfb", [128, SC], F32, kind="ExternalInput").ap()
    wob_i = nc.dram_tensor("wob", [128, DC], F32, kind="ExternalInput").ap()
    sgv_i = nc.dram_tensor("sgv", [128, HC], F32, kind="ExternalInput").ap()
    suv_i = nc.dram_tensor("suv", [128, HC], F32, kind="ExternalInput").ap()
    sdv_i = nc.dram_tensor("sdv", [128, DC], F32, kind="ExternalInput").ap()
    msk_i = nc.dram_tensor("msk", [128, NCORES], F32, kind="ExternalInput").ap()
    outT = nc.dram_tensor("outT", [D, TC], F32, kind="ExternalOutput").ap()
    spk = "ExternalOutput" if debug else "Internal"
    sp_f = nc.dram_tensor("sp_f", [S, TC], BF16, kind=spk).ap()
    sp_z = nc.dram_tensor("sp_z", [S, TC], BF16, kind=spk).ap()
    sp_og = nc.dram_tensor("sp_og", [S, TC], BF16, kind=spk).ap()
    sp_g = nc.dram_tensor("sp_g", [S, TC], BF16, kind=spk).ap()
    sp_woh = nc.dram_tensor("sp_woh", [DC, 128, S], BF16).ap()
    sp_wol = nc.dram_tensor("sp_wol", [DC, 128, S], BF16).ap()
    cc_in = nc.dram_tensor("cc_in", [2, 128, SC], F32)
    cc_out = nc.dram_tensor("cc_out", [2 * NCORES, 128, SC], F32)
    if debug:
        dbg_x1 = nc.dram_tensor("dbg_x1", [D, TC], F32, kind="ExternalOutput").ap()
        dbg_yr = nc.dram_tensor("dbg_yr", [D, TC], F32, kind="ExternalOutput").ap()
        dbg_y = nc.dram_tensor("dbg_y", [D, TC], F32, kind="ExternalOutput").ap()
        dbg_lp = nc.dram_tensor("dbg_lp", [4, 128, SC], F32, kind="ExternalOutput").ap()

    def sb(name, shape, dt, ctx=None):
        return (ctx or es).enter_context(nc.sbuf_tensor(name, shape, dt))

    ones32 = sb("ones32", [128, 128], F32)
    ones16 = sb("ones16", [128, 128], BF16)
    rmsw = sb("rmsw_s", [128, DC], F32)
    wfb = sb("wfb_s", [128, SC], F32)
    wob = sb("wob_s", [128, DC], F32)
    sgv = sb("sgv_s", [128, HC], F32)
    suv = sb("suv_s", [128, HC], F32)
    sdv = sb("sdv_s", [128, DC], F32)
    mskv = sb("msk_s", [128, NCORES], F32)
    cst = sb("cst", [128, 4], F32)
    LPl = sb("LPl", [128, SC], F32)
    LPp = sb("LPp", [128, SC], F32)
    Lst = sb("Lst", [128, SC], F32)
    Pcol = sb("Pcol", [128, SC], F32)
    ent = sb("ent", [128, SC], F32)
    ptmp = sb("ptmp", [128, SC], F32)
    ptm2 = sb("ptm2", [128, SC], F32)
    hstate = sb("hstate", [128, SC], F32)
    gl = [sb(f"gl{j}", [128, SC], F32) for j in range(NCORES - 1)]
    gp = [sb(f"gp{j}", [128, SC], F32) for j in range(NCORES - 1)]

    fzf = sb("fzf", [128, SC * TTB], BF16)
    fzz = sb("fzz", [128, SC * TTB], BF16)
    fog = sb("fog", [128, SC * TTB], BF16)
    hbuf = sb("hbuf", [128, 8 * TTB], F32)
    pb = [es.enter_context(nc.psum_tensor(f"pb{i}", [128, 512], F32))
          for i in range(8)]
    bank = [_Sem(es, nc, f"bank{i}") for i in range(8)]
    bank_uses = [0] * 8

    qx = _Sem(es, nc, "qx", 16)
    qw = _Sem(es, nc, "qw", 16)
    qsp = _Sem(es, nc, "qsp", 16)
    qcc = _Sem(es, nc, "qcc", 16)
    qfz = _Sem(es, nc, "qfz", 16)
    qgw = _Sem(es, nc, "qgw", 16)
    qgr = _Sem(es, nc, "qgr", 16)
    qwo = _Sem(es, nc, "qwo", 16)
    qxs = _Sem(es, nc, "qxs", 16)
    qfw = _Sem(es, nc, "qfw", 16)
    qout = _Sem(es, nc, "qout", 16)
    qvec = _Sem(es, nc, "qvec", 16)
    s_te = _Sem(es, nc, "s_te")
    s_dve = _Sem(es, nc, "s_dve")
    s_act = _Sem(es, nc, "s_act")
    s_cc = _Sem(es, nc, "s_cc")
    s_grd = _Sem(es, nc, "s_grd")
    s_xs = _Sem(es, nc, "s_xs")

    block = es.enter_context(nc.Block())
    E = {}

    def te_group(tensor, bank_i, mms, width):
        bank[bank_i].wait(tensor, bank_uses[bank_i])
        n = len(mms)
        last = None
        for i, (lh, rh) in enumerate(mms):
            last = tensor.matmul(pb[bank_i][:, 0:width], lh, rh,
                                 start=(i == 0), stop=(i == n - 1))
        return s_te.inc(last)

    def evac_done(inst, bank_i):
        bank_uses[bank_i] += 1
        inst.then_inc(bank[bank_i].sem, 1)
        bank[bank_i].n += 1

    def bank_evt(bank_i):
        return ("bank", bank_i, bank[bank_i].n)

    def wait_evt(eng, evt):
        kind, i, n = evt
        if kind == "bank":
            if n > 0:
                eng.wait_ge(bank[i].sem, n)
        elif kind == "xs":
            if n > 0:
                eng.wait_ge(s_xs.sem, n)
        elif kind == "grd":
            if n > 0:
                eng.wait_ge(s_grd.sem, n)

    @block.sync
    def _(sync):
        for dst, src in ((rmsw, rmsw_i), (wfb, wfb_i), (wob, wob_i),
                         (sgv, sgv_i), (suv, suv_i), (sdv, sdv_i),
                         (mskv, msk_i)):
            sync.dma_start(out=dst[:], in_=src).then_inc(qvec.sem, 16)
    qvec.n = 7 * 16

    E[("grd_scan_pre", 0, 0)] = 1
    @block.vector
    def _(vector):
        vector.memset(ones32[:], 1.0)
        vector.memset(ones16[:], 1.0)
        vector.memset(hstate[:], 0.0)
        vector.memset(ent[:], 0.0)
        vector.memset(Lst[:], 0.0)
        vector.memset(cst[:, 0:1], EPS)
        vector.memset(cst[:, 1:2], 1.0 / D)
        vector.memset(cst[:, 2:3], 2.0)
        vector.memset(cst[:, 3:4], -ZP)
        g = vector.engine_nop()
        s_grd.inc(g)
    E["grd_setup"] = s_grd.n

    ctxA = ExitStack()
    xstrm = [sb(f"xstrm{i}", [128, TJ], F32, ctxA) for i in range(2)]
    sqb = sb("sqb", [128, TJ], F32, ctxA)
    inv_rep = sb("inv_rep", [128, TJ], F32, ctxA)
    xnh = [sb(f"xnh{i}", [128, TJ], BF16, ctxA) for i in range(DC)]
    xnl = [sb(f"xnl{i}", [128, TJ], BF16, ctxA) for i in range(DC)]
    wland = [sb(f"wland{i}", [128, DC * SGW], F32, ctxA) for i in range(1)]
    wh_b = [sb(f"wh{i}", [128, DC * SGW], BF16, ctxA) for i in range(2)]
    wl_b = [sb(f"wl{i}", [128, DC * SGW], BF16, ctxA) for i in range(2)]
    bigA = [sb(f"bigA{c}", [128, TJ], F32, ctxA) for c in range(2)]
    preA = [sb(f"preA{c}", [128, TJ], F32, ctxA) for c in range(2)]
    f_bf = [sb(f"fbf{c}", [128, TJ], BF16, ctxA) for c in range(2)]
    z_bf = [sb(f"zbf{c}", [128, TJ], BF16, ctxA) for c in range(2)]
    og_bf = [sb(f"ogbf{c}", [128, TJ], BF16, ctxA) for c in range(2)]

    wo_split_after = []
    for i in range(SC):
        @block.gpsimd
        def _(gpsimd, i=i):
            if i >= 1:
                s_dve.wait(gpsimd, wo_split_after[i - 1])
            gpsimd.dma_start(out=wland[0][:, 0:D],
                             in_=woT[i * 128:(i + 1) * 128, :]
                             ).then_inc(qwo.sem, 16)
        qwo.n += 16
        E[("qwoland", i)] = qwo.n

        @block.vector
        def _(vector, i=i):
            qwo.wait(vector, E[("qwoland", i)])
            if i >= 2:
                qwo.wait(vector, E[("qwosp", i - 2)])
            vector.tensor_copy(wh_b[i % 2][:, 0:D], wland[0][:, 0:D])
            v = vector.tensor_tensor(wl_b[i % 2][:, 0:D], wland[0][:, 0:D],
                                     wh_b[i % 2][:, 0:D], ALU.subtract)
            s_dve.inc(v)
        wo_split_after.append(s_dve.n)

        @block.gpsimd
        def _(gpsimd, i=i):
            s_dve.wait(gpsimd, wo_split_after[i])
            for dcg in range(4):
                dst = sp_woh[dcg * 4:(dcg + 1) * 4, :, i * 128:(i + 1) * 128]
                dst = dst.rearrange("c p d -> p c d")
                srcv = wh_b[i % 2][:, dcg * 512:(dcg + 1) * 512]
                srcv = srcv.rearrange("p (c d) -> p c d", c=4)
                gpsimd.dma_start(out=dst, in_=srcv).then_inc(qwo.sem, 16)
                dst = sp_wol[dcg * 4:(dcg + 1) * 4, :, i * 128:(i + 1) * 128]
                dst = dst.rearrange("c p d -> p c d")
                srcv = wl_b[i % 2][:, dcg * 512:(dcg + 1) * 512]
                srcv = srcv.rearrange("p (c d) -> p c d", c=4)
                gpsimd.dma_start(out=dst, in_=srcv).then_inc(qwo.sem, 16)
        qwo.n += 8 * 16
        E[("qwosp", i)] = qwo.n
    E["qwo_staged"] = qwo.n

    PROJ = ("zg", "z", "f", "ga", "gb")
    nset = 0
    te_after_set = []
    split_after = []

    for j in range(NJ):
        tj0 = j * TJ

        for i in range(DC):
            @block.sync
            def _(sync, j=j, i=i, tj0=tj0):
                if j > 0 and i < 2:
                    s_dve.wait(sync, E[("xn_i", j - 1, DC - 2 + i)])
                if i >= 2:
                    s_act.wait(sync, E[("sq_i", j, i - 2)])
                sync.dma_start(out=xstrm[i % 2][:],
                               in_=xT[i * 128:(i + 1) * 128, tj0:tj0 + TJ]
                               ).then_inc(qx.sem, 16)
            qx.n += 16
            E[("qx", j, i)] = qx.n

            @block.scalar
            def _(scalar, j=j, i=i):
                if j == 0 and i == 0:
                    qvec.wait(scalar)
                    s_grd.wait(scalar, E["grd_setup"])
                qx.wait(scalar, E[("qx", j, i)])
                if j > 0 and i == 0:
                    s_dve.wait(scalar, E[("dve_xn", j - 1)])
                if i >= 2:
                    s_te.wait(scalar, s_te.n - 1)
                sqd = sqb[:, 0:TJ] if i % 2 == 0 else hbuf[:, TJ:2 * TJ]
                a = scalar.activation(sqd, xstrm[i % 2][:], AF.Square)
                s_act.inc(a)
            E[("sq_i", j, i)] = s_act.n

            @block.tensor
            def _(tensor, j=j, i=i):
                if j == 0 and i == 0:
                    s_grd.wait(tensor, E["grd_setup"])
                if i == 0:
                    for tt in range(NTA):
                        bank[tt].wait(tensor, bank_uses[tt])
                s_act.wait(tensor)
                sqd = sqb[:, 0:TJ] if i % 2 == 0 else hbuf[:, TJ:2 * TJ]
                last = None
                for tt in range(NTA):
                    last = tensor.matmul(pb[tt][:], ones32[:],
                                         sqd[:, tt * TTA:(tt + 1) * TTA],
                                         start=(i == 0), stop=(i == DC - 1))
                s_te.inc(last)

        @block.scalar
        def _(scalar, j=j):
            s_te.wait(scalar)
            for tt in range(NTA):
                a = scalar.activation(inv_rep[:, tt * TTA:(tt + 1) * TTA],
                                      pb[tt][:], AF.Sqrt, bias=cst[:, 0:1],
                                      scale=cst[:, 1:2])
                evac_done(a, tt)
        E[("act_inv", j)] = bank_evt(NTA - 1)

        for i in range(DC):
            @block.sync
            def _(sync, j=j, i=i, tj0=tj0):
                if i == 0:
                    wait_evt(sync, E[("act_inv", j)])
                if i >= 2:
                    s_dve.wait(sync, E[("xn_i", j, i - 2)])
                sync.dma_start(out=xstrm[i % 2][:],
                               in_=xT[i * 128:(i + 1) * 128, tj0:tj0 + TJ]
                               ).then_inc(qx.sem, 16)
            qx.n += 16
            E[("qx1", j, i)] = qx.n

            @block.vector
            def _(vector, j=j, i=i):
                if i == 0:
                    wait_evt(vector, E[("act_inv", j)])
                    if j == 0:
                        qvec.wait(vector)
                    vector.reciprocal(sqb[:], inv_rep[:])
                    vector.tensor_copy(inv_rep[:], sqb[:])
                qx.wait(vector, E[("qx1", j, i)])
                vector.scalar_tensor_tensor(sqb[:], xstrm[i % 2][:],
                                            rmsw[:, i:i + 1], inv_rep[:],
                                            ALU.mult, ALU.mult)
                vector.tensor_copy(xnh[i][:], sqb[:])
                v = vector.tensor_tensor(xnl[i][:], sqb[:], xnh[i][:],
                                         ALU.subtract)
                s_dve.inc(v)
            E[("xn_i", j, i)] = s_dve.n
        E[("dve_xn", j)] = s_dve.n

        for sg in range(NSG):
            for pi, kind in enumerate(PROJ):
                idx = nset
                buf = idx % 2
                w_ap = wpT[pi]

                @block.sync
                def _(sync, sg=sg, idx=idx, buf=buf, w_ap=w_ap):
                    if idx < 2:
                        qwo.wait(sync, E["qwo_staged"])
                    if idx >= 1:
                        s_dve.wait(sync, split_after[idx - 1])
                    src = w_ap[:, sg * SGW:(sg + 1) * SGW]
                    src = src.rearrange("(c p) s -> p c s", p=128)
                    dst = wland[0][:].rearrange("p (c s) -> p c s", c=DC)
                    sync.dma_start(out=dst, in_=src).then_inc(qw.sem, 16)
                qw.n += 16

                @block.vector
                def _(vector, idx=idx, buf=buf):
                    qw.wait(vector, (idx + 1) * 16)
                    if idx < 2:
                        qwo.wait(vector, E["qwo_staged"])
                    if idx >= 2:
                        s_te.wait(vector, te_after_set[idx - 2])
                    vector.tensor_copy(wh_b[buf][:], wland[0][:])
                    v = vector.tensor_tensor(wl_b[buf][:], wland[0][:],
                                             wh_b[buf][:], ALU.subtract)
                    s_dve.inc(v)
                split_after.append(s_dve.n)

                @block.tensor
                def _(tensor, j=j, idx=idx, buf=buf):
                    s_dve.wait(tensor, split_after[idx])
                    s_dve.wait(tensor, E[("dve_xn", j)])
                    for c in range(2):
                        for tt in range(NTA):
                            g = ((idx * 2 + c) * NTA + tt) % 8
                            mms = []
                            for kk in range(DC):
                                lh_h = wh_b[buf][:, kk * SGW + c * 128:
                                                 kk * SGW + (c + 1) * 128]
                                lh_l = wl_b[buf][:, kk * SGW + c * 128:
                                                 kk * SGW + (c + 1) * 128]
                                rh_h = xnh[kk][:, tt * TTA:(tt + 1) * TTA]
                                rh_l = xnl[kk][:, tt * TTA:(tt + 1) * TTA]
                                mms += [(lh_h, rh_h), (lh_h, rh_l),
                                        (lh_l, rh_h)]
                            te_group(tensor, g, mms, TTA)
                te_after_set.append(s_te.n)

                @block.scalar
                def _(scalar, j=j, sg=sg, idx=idx, kind=kind):
                    base = te_after_set[idx] - 2 * NTA
                    if kind == "zg" and (j, sg) != (0, 0):
                        qsp.wait(scalar, E[("qsp_need", j, sg)])
                        s_dve.wait(scalar, E[("scan_prev", j, sg)])
                    if kind in ("ga", "gb"):
                        s_dve.wait(scalar, E[("comb", j, sg, "z")])
                    for c in range(2):
                        ci = sg * 2 + c
                        for tt in range(NTA):
                            gi = c * NTA + tt
                            g = ((idx * 2 + c) * NTA + tt) % 8
                            s_te.wait(scalar, base + gi + 1)
                            psl = pb[g][:, 0:TTA]
                            dst = slice(tt * TTA, (tt + 1) * TTA)
                            if kind == "zg":
                                a = scalar.activation(bigA[c][:, dst], psl,
                                                      AF.Sigmoid)
                            elif kind == "z":
                                a = scalar.activation(preA[c][:, dst], psl,
                                                      AF.Copy)
                            elif kind == "f":
                                a = scalar.activation(f_bf[c][:, dst], psl,
                                                      AF.Sigmoid,
                                                      bias=wfb[:, ci:ci + 1])
                            elif kind == "ga":
                                a = scalar.activation(bigA[c][:, dst], psl,
                                                      AF.Silu)
                            else:
                                a = scalar.activation(preA[c][:, dst], psl,
                                                      AF.Copy)
                            evac_done(a, g)
                last_g = ((idx * 2 + 1) * NTA + NTA - 1) % 8
                E[("evac", j, sg, kind)] = bank_evt(last_g)

                if kind in ("z", "gb"):
                    @block.vector
                    def _(vector, j=j, sg=sg, kind=kind):
                        wait_evt(vector, E[("evac", j, sg, kind)])
                        if (j, sg) != (0, 0):
                            qsp.wait(vector, E[("qsp_need", j, sg)])
                        for c in range(2):
                            dst = z_bf[c] if kind == "z" else og_bf[c]
                            v = vector.tensor_tensor(dst[:], preA[c][:],
                                                     bigA[c][:], ALU.mult)
                        s_dve.inc(v)
                    E[("comb", j, sg, kind)] = s_dve.n
                nset += 1

            @block.vector
            def _(vector, j=j, sg=sg):
                wait_evt(vector, E[("evac", j, sg, "f")])
                s_grd.wait(vector, E[("grd_scan_pre", j, sg)])

                def grd(inst=None):
                    g = vector.engine_nop()
                    s_grd.inc(g)
                    vector.wait_ge(s_grd.sem, s_grd.n)

                for c in range(2):
                    ci = sg * 2 + c
                    vector.tensor_tensor_scan(hbuf[:, 0:TJ], f_bf[c][:], z_bf[c][:],
                                              Lst[:, ci:ci + 1],
                                              ALU.mult, ALU.add)
                    grd()
                    if j == 0:
                        vector.tensor_copy(Lst[:, ci:ci + 1],
                                           hbuf[:, TJ - 1:TJ])
                    else:
                        vector.tensor_copy(LPl[:, ci:ci + 1],
                                           hbuf[:, TJ - 1:TJ])
                    grd()
                    vector.tensor_reduce(Pcol[:, ci:ci + 1], f_bf[c][:],
                                         AX.X, ALU.mult)
                    grd()
                    if j == 0:
                        v = vector.tensor_copy(LPp[:, ci:ci + 1],
                                               Pcol[:, ci:ci + 1])
                    else:
                        v = vector.tensor_tensor(LPp[:, ci:ci + 1],
                                                 LPp[:, ci:ci + 1],
                                                 Pcol[:, ci:ci + 1], ALU.mult)
                    grd()
                s_dve.inc(v)
            E[("scan", j, sg)] = s_dve.n
            E[("grd_scan_pre", j, sg + 1)] = s_grd.n
            if sg == NSG - 1 and j == 0:
                E[("grd_scan_pre", 1, 0)] = s_grd.n
            E[("scan_prev", j, sg + 1)] = s_dve.n
            if sg == NSG - 1 and j == 0:
                E[("scan_prev", 1, 0)] = s_dve.n

            @block.gpsimd
            def _(gpsimd, j=j, sg=sg, tj0=tj0):
                s_dve.wait(gpsimd, E[("scan", j, sg)])
                for c in range(2):
                    ci = sg * 2 + c
                    rows = slice(ci * 128, (ci + 1) * 128)
                    tcols = slice(tj0, tj0 + TJ)
                    gpsimd.dma_start(out=sp_f[rows, tcols], in_=f_bf[c][:]
                                     ).then_inc(qsp.sem, 16)
                    gpsimd.dma_start(out=sp_z[rows, tcols], in_=z_bf[c][:]
                                     ).then_inc(qsp.sem, 16)
                    gpsimd.dma_start(out=sp_og[rows, tcols], in_=og_bf[c][:]
                                     ).then_inc(qsp.sem, 16)
            qsp.n += 6 * 16
            E[("qsp_need", j, sg + 1)] = qsp.n
            if sg == NSG - 1 and j == 0:
                E[("qsp_need", 1, 0)] = qsp.n

        @block.vector
        def _(vector, j=j):
            g = vector.engine_nop()
            s_grd.inc(g)

    @block.gpsimd
    def _(gpsimd):
        qsp.wait(gpsimd, E[("qsp_need", 1, 0)])
        for (srct, dstt) in ((sp_f, fzf), (sp_z, fzz), (sp_og, fog)):
            for q in range(4):
                rows = srct[q * 512:(q + 1) * 512, 0:TTB]
                rows = rows.rearrange("(c p) t -> p c t", p=128)
                dv = dstt[:, q * 4 * TTB:(q + 1) * 4 * TTB]
                dv = dv.rearrange("p (c t) -> p c t", c=4)
                gpsimd.dma_start(out=dv, in_=rows).then_inc(qfz.sem, 16)
    qfz.n += 12 * 16
    E[("qfz", 0)] = qfz.n

    @block.gpsimd
    def _(gpsimd):
        s_dve.wait(gpsimd, E[("scan", NJ - 1, NSG - 1)])
        gpsimd.dma_start(
            out=cc_in.ap()[0:1].rearrange("o p c -> p o c"),
            in_=LPl[:].unsqueeze(1)).then_inc(qcc.sem, 16)
        gpsimd.dma_start(
            out=cc_in.ap()[1:2].rearrange("o p c -> p o c"),
            in_=LPp[:].unsqueeze(1)).then_inc(qcc.sem, 16)
        gpsimd.wait_ge(qcc.sem, 32)
        cc = gpsimd.collective_compute(
            "AllGather", ALU.bypass, replica_groups=[list(range(NCORES))],
            ins=[cc_in.ap().opt()], outs=[cc_out.ap().opt()])
        s_cc.inc(cc)
        s_cc.wait(gpsimd)
        for jj in range(NCORES - 1):
            gpsimd.dma_start(
                out=gl[jj][:].unsqueeze(1),
                in_=cc_out.ap()[2 * jj:2 * jj + 1].rearrange(
                    "o p c -> p o c")).then_inc(qcc.sem, 16)
            gpsimd.dma_start(
                out=gp[jj][:].unsqueeze(1),
                in_=cc_out.ap()[2 * jj + 1:2 * jj + 2].rearrange(
                    "o p c -> p o c")).then_inc(qcc.sem, 16)
    qcc.n = 32 + (NCORES - 1) * 32

    @block.vector
    def _(vector):
        qcc.wait(vector)

        def grd():
            g = vector.engine_nop()
            s_grd.inc(g)
            vector.wait_ge(s_grd.sem, s_grd.n)

        for jj in range(NCORES - 1):
            vector.tensor_tensor(ptmp[:], gp[jj][:], ent[:], ALU.mult)
            grd()
            vector.tensor_tensor(ptmp[:], ptmp[:], gl[jj][:], ALU.add)
            grd()
            vector.tensor_tensor(ptm2[:], ptmp[:], ent[:], ALU.subtract)
            grd()
            vector.scalar_tensor_tensor(ent[:], ptm2[:], mskv[:, jj:jj + 1],
                                        ent[:], ALU.mult, ALU.add)
            grd()
        vector.tensor_copy(hstate[:], ent[:])
        g = vector.engine_nop()
        s_grd.inc(g)
    E["grd_entry"] = s_grd.n

    te_A, dve_A, act_A = s_te.n, s_dve.n, s_act.n
    ctxA.close()

    ctxB = ExitStack()


    wob_h = [sb(f"wobh{i}", [128, S], BF16, ctxB) for i in range(2)]
    wob_l = [sb(f"wobl{i}", [128, S], BF16, ctxB) for i in range(2)]
    gated = [sb(f"gated{i}", [128, SC * TTB], BF16, ctxB) for i in range(2)]
    xsb = sb("xsb", [128, DC * TTB], F32, ctxB)
    big_a = [sb(f"biga{i}", [128, TTB], F32, ctxB) for i in range(DC)]
    vb = [sb(f"vb{i}", [128, TTB], F32, ctxB) for i in range(DC)]
    x1f = [sb(f"x1f{i}", [128, TTB], F32, ctxB) for i in range(DC)]
    x1h = [sb(f"x1h{i}", [128, TTB], BF16, ctxB) for i in range(DC)]
    x1l = [sb(f"x1l{i}", [128, TTB], BF16, ctxB) for i in range(DC)]
    acth = [sb(f"acth{i}", [128, TTB], BF16, ctxB) for i in range(22)]
    actl = [sb(f"actl{i}", [128, TTB], BF16, ctxB) for i in range(22)]
    sqt = sb("sqt", [128, TTB], F32, ctxB)
    sqt2 = sb("sqt2", [128, TTB], F32, ctxB)
    mu_r = sb("mu_r", [128, TTB], F32, ctxB)
    m2_r = sb("m2_r", [128, TTB], F32, ctxB)
    iv_r = sb("iv_r", [128, TTB], F32, ctxB)
    csx = sb("csx", [128, TTB], F32, ctxB)
    csa = sb("csa", [128, TTB], F32, ctxB)
    gt_f = sb("gt_f", [128, TTB], F32, ctxB)
    up_f = sb("up_f", [128, TTB], F32, ctxB)
    af_f = sb("af_f", [128, TTB], F32, ctxB)
    ge_f = sb("ge_f", [128, TTB], F32, ctxB)
    ue_f = sb("ue_f", [128, TTB], F32, ctxB)
    wgt = [sb(f"wgt{i}", [128, DC * 128], BF16, ctxB) for i in range(2)]
    wut = [sb(f"wut{i}", [128, DC * 128], BF16, ctxB) for i in range(2)]
    wdt = [sb(f"wdt{i}", [128, 8 * 128], BF16, ctxB) for i in range(2)]
    outb = [sb(f"outb{i}", [128, TTB], F32, ctxB) for i in range(3)]

    def emit_bpre(k):
        t0 = k * TTB
        gb_ = gated[k % 2]
        if k > 0:
            @block.gpsimd
            def _(gpsimd, k=k, t0=t0):
                if k >= 4:
                    qsp.wait(gpsimd)
                s_grd.wait(gpsimd, E[("bscan_grd", k - 1)])
                for (srct, dstt) in ((sp_f, fzf), (sp_z, fzz), (sp_og, fog)):
                    for q in range(4):
                        rows = srct[q * 512:(q + 1) * 512, t0:t0 + TTB]
                        rows = rows.rearrange("(c p) t -> p c t", p=128)
                        dv = dstt[:, q * 4 * TTB:(q + 1) * 4 * TTB]
                        dv = dv.rearrange("p (c t) -> p c t", c=4)
                        gpsimd.dma_start(out=dv, in_=rows).then_inc(qfz.sem, 16)
            qfz.n += 12 * 16
            E[("qfz", k)] = qfz.n

        @block.vector
        def _(vector, k=k, gb_=gb_):
            qfz.wait(vector, E[("qfz", k)])
            s_grd.wait(vector, E["grd_entry"] if k == 0
                       else E[("bscan_grd", k - 1)])
            if k >= 2:
                s_te.wait(vector, E[("wout_te", k - 2)])
            for half8 in range(2):
                for cc8 in range(8):
                    c = half8 * 8 + cc8
                    cs = slice(c * TTB, (c + 1) * TTB)
                    hs = slice(cc8 * TTB, (cc8 + 1) * TTB)
                    vector.tensor_tensor_scan(hbuf[:, hs], fzf[:, cs],
                                              fzz[:, cs], hstate[:, c:c + 1],
                                              ALU.mult, ALU.add)
                    vector.tensor_tensor(gb_[:, cs], fog[:, cs], hbuf[:, hs],
                                         ALU.mult)
                g = vector.engine_nop()
                s_grd.inc(g)
                vector.wait_ge(s_grd.sem, s_grd.n)
                vector.tensor_copy(hstate[:, half8 * 8:half8 * 8 + 8],
                                   hbuf[:, TTB - 1:8 * TTB:TTB])
                g = vector.engine_nop()
                s_grd.inc(g)
                vector.wait_ge(s_grd.sem, s_grd.n)
        E[("bscan_grd", k)] = s_grd.n

        if debug:
            @block.gpsimd
            def _(gpsimd, k=k, t0=t0, gb_=gb_):
                s_grd.wait(gpsimd, E[("bscan_grd", k)])
                for q in range(4):
                    rows = sp_g[q * 512:(q + 1) * 512, t0:t0 + TTB]
                    rows = rows.rearrange("(c p) t -> p c t", p=128)
                    sv = gb_[:, q * 4 * TTB:(q + 1) * 4 * TTB]
                    sv = sv.rearrange("p (c t) -> p c t", c=4)
                    gpsimd.dma_start(out=rows, in_=sv).then_inc(qgw.sem, 16)
            qgw.n += 64


    emit_bpre(0)

    for k in range(NTB):
        t0 = k * TTB
        gb_ = gated[k % 2]

        pass
        for dc in range(DC):
            @block.sync
            def _(sync, k=k, dc=dc):
                seq = 16 * k + dc
                if k == 0 and dc == 0:
                    qwo.wait(sync, E["qwo_staged"])
                    s_te.wait(sync, te_A)
                    s_dve.wait(sync, dve_A)
                    s_act.wait(sync, act_A)
                    qsp.wait(sync)
                if seq >= 2:
                    s_te.wait(sync, E.get(("wout_dc", (seq - 2) // 16,
                                           (seq - 2) % 16), 0))
                sync.dma_start(out=wob_h[dc % 2][:], in_=sp_woh[dc]
                               ).then_inc(qwo.sem, 16)
                sync.dma_start(out=wob_l[dc % 2][:], in_=sp_wol[dc]
                               ).then_inc(qwo.sem, 16)
            qwo.n += 32
            E[("qwob", k, dc)] = qwo.n

            @block.tensor
            def _(tensor, k=k, dc=dc, gb_=gb_):
                if dc == 0:
                    s_grd.wait(tensor, E[("bscan_grd", k)])
                qwo.wait(tensor, E[("qwob", k, dc)])
                mms = []
                for c in range(SC):
                    cs = slice(c * TTB, (c + 1) * TTB)
                    mms += [(wob_h[dc % 2][:, c * 128:(c + 1) * 128], gb_[:, cs]),
                            (wob_l[dc % 2][:, c * 128:(c + 1) * 128], gb_[:, cs])]
                te_group(tensor, dc % 8, mms, TTB)
            E[("wout_dc", k, dc)] = s_te.n
        E[("wout_te", k)] = s_te.n

        @block.sync
        def _(sync, k=k, t0=t0):
            if k >= 1:
                s_xs.wait(sync, 16 * k)
            for q in range(4):
                rows = xT[q * 512:(q + 1) * 512, t0:t0 + TTB]
                rows = rows.rearrange("(c p) t -> p c t", p=128)
                dv = xsb[:, q * 4 * TTB:(q + 1) * 4 * TTB]
                dv = dv.rearrange("p (c t) -> p c t", c=4)
                sync.dma_start(out=dv, in_=rows).then_inc(qxs.sem, 16)
            qxs.n += 64
        E[("qxs", k)] = qxs.n

        @block.scalar
        def _(scalar, k=k, t0=t0):
            base = E[("wout_te", k)] - DC
            if k >= 1:
                s_dve.wait(scalar, E[("out", k - 1, DC - 1)])
                if debug:
                    qcc.wait(scalar)
            for dc in range(DC):
                s_te.wait(scalar, base + dc + 1)
                a = scalar.activation(big_a[dc][:], pb[dc % 8][:, 0:TTB],
                                      AF.Identity, bias=wob[:, dc:dc + 1])
                evac_done(a, dc % 8)
                E[("yev_dc", k, dc)] = bank_evt(dc % 8)

        @block.vector
        def _(vector, k=k):
            for dc in range(DC):
                wait_evt(vector, E[("yev_dc", k, dc)])
                if dc == 0:
                    qxs.wait(vector, E[("qxs", k)])
                v = vector.scalar_tensor_tensor(
                    vb[dc][:], xsb[:, dc * TTB:(dc + 1) * TTB],
                    cst[:, 2:3], big_a[dc][:], ALU.mult, ALU.add)
                s_xs.inc(v)
        E[("v", k)] = ("xs", 0, s_xs.n)

        def emit_ln(tag, k, src, base_key, per_dc):
            sq_b = (sqt, sqt2)
            for dc in range(DC):
                @block.scalar
                def _(scalar, dc=dc):
                    if per_dc:
                        kind0, _, n0 = E[(base_key, k)]
                        assert kind0 == "xs"
                        scalar.wait_ge(s_xs.sem, n0 - DC + dc + 1)
                    else:
                        s_dve.wait(scalar, E[(base_key, k)])
                    if dc >= 2:
                        s_te.wait(scalar, s_te.n - 1)
                    a = scalar.activation(sq_b[dc % 2][:], src[dc][:], AF.Square)
                    s_act.inc(a)

                @block.tensor
                def _(tensor, dc=dc):
                    s_act.wait(tensor)
                    if dc == 0:
                        bank[6].wait(tensor, bank_uses[6])
                        bank[7].wait(tensor, bank_uses[7])
                    tensor.matmul(pb[6][:, 0:TTB], ones32[:], src[dc][:],
                                  start=(dc == 0), stop=(dc == DC - 1))
                    last = tensor.matmul(pb[7][:, 0:TTB], ones32[:],
                                         sq_b[dc % 2][:],
                                         start=(dc == 0), stop=(dc == DC - 1))
                    s_te.inc(last)

            @block.scalar
            def _(scalar):
                s_te.wait(scalar)
                a = scalar.activation(mu_r[:], pb[6][:, 0:TTB], AF.Identity,
                                      scale=cst[:, 1:2])
                evac_done(a, 6)
                a = scalar.activation(m2_r[:], pb[7][:, 0:TTB], AF.Identity,
                                      scale=cst[:, 1:2])
                evac_done(a, 7)
            E[(tag + "_ms", k)] = bank_evt(7)

            @block.vector
            def _(vector):
                wait_evt(vector, E[(tag + "_ms", k)])
                vector.tensor_tensor(sqt[:], mu_r[:], mu_r[:], ALU.mult)
                v = vector.tensor_tensor(m2_r[:], m2_r[:], sqt[:], ALU.subtract)
                s_dve.inc(v)
            E[(tag + "_var", k)] = s_dve.n

            @block.scalar
            def _(scalar):
                s_dve.wait(scalar, E[(tag + "_var", k)])
                a = scalar.activation(iv_r[:], m2_r[:], AF.Sqrt,
                                      bias=cst[:, 0:1])
                s_act.inc(a)
            E[(tag + "_sqrt", k)] = s_act.n

            @block.vector
            def _(vector):
                s_act.wait(vector, E[(tag + "_sqrt", k)])
                vector.reciprocal(m2_r[:], iv_r[:])
                v = vector.tensor_copy(iv_r[:], m2_r[:])
                s_dve.inc(v)
            E[(tag + "_inv", k)] = s_dve.n

        emit_ln("ln1", k, vb, "v", per_dc=True)

        @block.vector
        def _(vector, k=k):
            s_dve.wait(vector, E[("ln1_inv", k)])
            for dc in range(DC):
                vector.tensor_tensor(x1f[dc][:], vb[dc][:], mu_r[:],
                                     ALU.subtract)
                vector.tensor_tensor(x1f[dc][:], x1f[dc][:], iv_r[:], ALU.mult)
                vector.tensor_copy(x1h[dc][:], x1f[dc][:])
                v = vector.tensor_tensor(x1l[dc][:], x1f[dc][:], x1h[dc][:],
                                         ALU.subtract)
            s_dve.inc(v)
        E[("x1", k)] = s_dve.n

        if k + 1 < NTB:
            emit_bpre(k + 1)

        if debug:
            @block.gpsimd
            def _(gpsimd, k=k):
                s_dve.wait(gpsimd, E[("x1", k)])
                for dc in range(DC):
                    gpsimd.dma_start(
                        out=dbg_x1[dc * 128:(dc + 1) * 128,
                                   k * TTB:(k + 1) * TTB],
                        in_=x1f[dc][:]).then_inc(qcc.sem, 16)
                    gpsimd.dma_start(
                        out=dbg_y[dc * 128:(dc + 1) * 128,
                                  k * TTB:(k + 1) * TTB],
                        in_=vb[dc][:]).then_inc(qcc.sem, 16)
                    gpsimd.dma_start(
                        out=dbg_yr[dc * 128:(dc + 1) * 128,
                                   k * TTB:(k + 1) * TTB],
                        in_=big_a[dc][:]).then_inc(qcc.sem, 16)
            qcc.n += 48 * DC

        @block.tensor
        def _(tensor, k=k):
            s_dve.wait(tensor, E[("x1", k)])
            mms = []
            for dc in range(DC):
                mms += [(ones16[:], x1h[dc][:]), (ones16[:], x1l[dc][:])]
            te_group(tensor, 2, mms, TTB)
        E[("csx_te", k)] = s_te.n

        @block.scalar
        def _(scalar, k=k):
            s_te.wait(scalar, E[("csx_te", k)])
            a = scalar.activation(csx[:], pb[2][:, 0:TTB], AF.Copy)
            evac_done(a, 2)
        E[("csx", k)] = bank_evt(2)

        for half in range(2):
            hlo = half * 22
            for hh in range(22):
                hc = hlo + hh

                @block.gpsimd
                def _(gpsimd, k=k, hc=hc):
                    buf = hc % 2
                    if ("gu_te", k, hc - 2) in E:
                        s_te.wait(gpsimd, E[("gu_te", k, hc - 2)])
                    elif k > 0 and hc < 2:
                        s_te.wait(gpsimd, E[("gu_te", k - 1, 42 + hc)])
                    srct = wgT4[hc:hc + 1].rearrange("o c p h -> p (o c) h")
                    dv = wgt[buf][:].rearrange("p (c h) -> p c h", c=DC)
                    gpsimd.dma_start(out=dv, in_=srct).then_inc(qfw.sem, 16)
                    srct = wuT4[hc:hc + 1].rearrange("o c p h -> p (o c) h")
                    dv = wut[buf][:].rearrange("p (c h) -> p c h", c=DC)
                    gpsimd.dma_start(out=dv, in_=srct).then_inc(qfw.sem, 16)
                qfw.n += 32
                E[("gu_w", k, hc)] = qfw.n

                @block.tensor
                def _(tensor, k=k, hc=hc, hh=hh, half=half):
                    buf = hc % 2
                    qfw.wait(tensor, E[("gu_w", k, hc)])
                    if hh == 0 and half == 0:
                        wait_evt(tensor, E[("csx", k)])
                    mms_g, mms_u = [], []
                    for dc in range(DC):
                        wsl = slice(dc * 128, (dc + 1) * 128)
                        mms_g += [(wgt[buf][:, wsl], x1h[dc][:]),
                                  (wgt[buf][:, wsl], x1l[dc][:])]
                        mms_u += [(wut[buf][:, wsl], x1h[dc][:]),
                                  (wut[buf][:, wsl], x1l[dc][:])]
                    gbk, ubk = (3, 4) if hc % 2 == 0 else (5, 6)
                    te_group(tensor, gbk, mms_g, TTB)
                    te_group(tensor, ubk, mms_u, TTB)
                E[("gu_te", k, hc)] = s_te.n

                @block.vector
                def _(vector, k=k, hc=hc):
                    s_te.wait(vector, E[("gu_te", k, hc)])
                    if ("gu_act", k, hc - 1) in E:
                        s_act.wait(vector, E[("gu_act", k, hc - 1)])
                    gbk, ubk = (3, 4) if hc % 2 == 0 else (5, 6)
                    v = vector.scalar_tensor_tensor(ge_f[:], csx[:],
                                                    cst[:, 3:4],
                                                    pb[gbk][:, 0:TTB],
                                                    ALU.mult, ALU.add)
                    evac_done(v, gbk)
                    v = vector.scalar_tensor_tensor(ue_f[:], csx[:],
                                                    cst[:, 3:4],
                                                    pb[ubk][:, 0:TTB],
                                                    ALU.mult, ALU.add)
                    evac_done(v, ubk)
                E[("gu_ev", k, hc)] = bank_evt(4 if hc % 2 == 0 else 6)

                @block.scalar
                def _(scalar, k=k, hc=hc):
                    wait_evt(scalar, E[("gu_ev", k, hc)])
                    scalar.activation(gt_f[:], ge_f[:], AF.Silu,
                                      scale=sgv[:, hc:hc + 1])
                    a = scalar.activation(up_f[:], ue_f[:], AF.Identity,
                                          scale=suv[:, hc:hc + 1])
                    s_act.inc(a)
                E[("gu_act", k, hc)] = s_act.n

                @block.vector
                def _(vector, k=k, hc=hc, hh=hh):
                    s_act.wait(vector, E[("gu_act", k, hc)])
                    vector.tensor_tensor(af_f[:], gt_f[:], up_f[:], ALU.mult)
                    vector.tensor_copy(acth[hh][:], af_f[:])
                    v = vector.tensor_tensor(actl[hh][:], af_f[:], acth[hh][:],
                                             ALU.subtract)
                    s_dve.inc(v)
                E[("act", k, hc)] = s_dve.n

            @block.tensor
            def _(tensor, k=k, hlo=hlo):
                s_dve.wait(tensor, E[("act", k, hlo + 21)])
                mms = []
                for hh in range(22):
                    mms += [(ones16[:], acth[hh][:]), (ones16[:], actl[hh][:])]
                te_group(tensor, 2, mms, TTB)
            E[("csa_te", k, half)] = s_te.n

            @block.scalar
            def _(scalar, k=k, half=half):
                s_te.wait(scalar, E[("csa_te", k, half)])
                if half == 1:
                    wait_evt(scalar, E[("dn_ev", k, 0, 1)])
                a = scalar.activation(csa[:], pb[2][:, 0:TTB], AF.Copy)
                evac_done(a, 2)
            E[("csa", k, half)] = bank_evt(2)

            for dh in range(2):
                for hh in range(22):
                    hc = hlo + hh

                    @block.gpsimd
                    def _(gpsimd, k=k, half=half, dh=dh, hh=hh, hc=hc):
                        buf = hh % 2
                        if ("dn_te", k, half, dh, hh - 2) in E:
                            s_te.wait(gpsimd, E[("dn_te", k, half, dh, hh - 2)])
                        elif hh < 2:
                            for key in (("dn_te", k, half, dh - 1, 20 + hh),
                                        ("dn_te", k, half - 1, 1, 20 + hh),
                                        ("dn_te", k - 1, 1, 1, 20 + hh)):
                                if key in E:
                                    s_te.wait(gpsimd, E[key])
                                    break
                        srct = wdT4[hc:hc + 1, dh * 8:(dh + 1) * 8]
                        srct = srct.rearrange("o c p d -> p (o c) d")
                        dv = wdt[buf][:].rearrange("p (c d) -> p c d", c=8)
                        gpsimd.dma_start(out=dv, in_=srct).then_inc(qfw.sem, 16)
                    qfw.n += 16
                    E[("dn_w", k, half, dh, hh)] = qfw.n

                    @block.tensor
                    def _(tensor, k=k, half=half, dh=dh, hh=hh):
                        buf = hh % 2
                        qfw.wait(tensor, E[("dn_w", k, half, dh, hh)])
                        if hh == 0:
                            for b8 in range(8):
                                bank[b8].wait(tensor, bank_uses[b8])
                        last = None
                        for dc8 in range(8):
                            wsl = slice(dc8 * 128, (dc8 + 1) * 128)
                            last = tensor.matmul(pb[dc8][:, 0:TTB],
                                                 wdt[buf][:, wsl], acth[hh][:],
                                                 start=(hh == 0), stop=False)
                            last = tensor.matmul(pb[dc8][:, 0:TTB],
                                                 wdt[buf][:, wsl], actl[hh][:],
                                                 start=False, stop=(hh == 21))
                        s_te.inc(last)
                    E[("dn_te", k, half, dh, hh)] = s_te.n

                @block.vector
                def _(vector, k=k, half=half, dh=dh):
                    s_te.wait(vector, E[("dn_te", k, half, dh, 21)])
                    if debug:
                        qcc.wait(vector)
                    wait_evt(vector, E[("csa", k, half)])
                    for dc8 in range(8):
                        dc = dh * 8 + dc8
                        vector.scalar_tensor_tensor(sqt[:], csa[:],
                                                    cst[:, 3:4],
                                                    pb[dc8][:, 0:TTB],
                                                    ALU.mult, ALU.add)
                        if half == 0:
                            v = vector.tensor_copy(big_a[dc][:], sqt[:])
                        else:
                            v = vector.tensor_tensor(big_a[dc][:], big_a[dc][:],
                                                     sqt[:], ALU.add)
                        evac_done(v, dc8)
                E[("dn_ev", k, half, dh)] = bank_evt(7)

        @block.vector
        def _(vector, k=k):
            if debug:
                qcc.wait(vector)
            v = None
            for dc in range(DC):
                v = vector.scalar_tensor_tensor(vb[dc][:], big_a[dc][:],
                                                sdv[:, dc:dc + 1], x1f[dc][:],
                                                ALU.mult, ALU.add)
            s_dve.inc(v)
        E[("v2", k)] = s_dve.n

        emit_ln("ln2", k, vb, "v2", per_dc=False)

        for dc in range(DC):
            @block.vector
            def _(vector, k=k, dc=dc):
                if dc == 0:
                    s_dve.wait(vector, E[("ln2_inv", k)])
                seq = 16 * k + dc
                if seq >= 3:
                    qout.wait(vector, E[("outdma", (seq - 3) // 16,
                                         (seq - 3) % 16)])
                ob = outb[dc % 3]
                vector.tensor_tensor(ob[:], vb[dc][:], mu_r[:], ALU.subtract)
                v = vector.tensor_tensor(ob[:], ob[:], iv_r[:], ALU.mult)
                s_dve.inc(v)
            E[("out", k, dc)] = s_dve.n

            @block.scalar
            def _(scalar, k=k, dc=dc):
                s_dve.wait(scalar, E[("out", k, dc)])
                scalar.dma_start(out=outT[dc * 128:(dc + 1) * 128,
                                          k * TTB:(k + 1) * TTB],
                                 in_=outb[dc % 3][:]).then_inc(qout.sem, 16)
            qout.n += 16
            E[("outdma", k, dc)] = qout.n

    if debug:
        @block.gpsimd
        def _(gpsimd):
            qout.wait(gpsimd)
            gpsimd.dma_start(out=dbg_lp[0], in_=LPl[:]).then_inc(qcc.sem, 16)
            gpsimd.dma_start(out=dbg_lp[1], in_=LPp[:]).then_inc(qcc.sem, 16)
            gpsimd.dma_start(out=dbg_lp[2], in_=ent[:]).then_inc(qcc.sem, 16)
            gpsimd.dma_start(out=dbg_lp[3], in_=hstate[:]).then_inc(qcc.sem, 16)
            qcc.n += 4 * 16
            qcc.wait(gpsimd)

    @block.scalar
    def _(scalar):
        qout.wait(scalar)

    @block.gpsimd
    def _(gpsimd):
        qout.wait(gpsimd)

    ctxB.close()
    es.close()
    return nc


def _prep_inputs(L0_attn_x, L0_attn_rms_w, L0_attn_Wf_w, L0_attn_Wf_b,
                 L0_attn_Wzg_w, L0_attn_Wz_w, L0_attn_Wga_w, L0_attn_Wgb_w,
                 L0_attn_Wout_w, L0_attn_Wout_b, L0_ffn_wq_gate, L0_ffn_s_gate,
                 L0_ffn_wq_up, L0_ffn_s_up, L0_ffn_wq_down, L0_ffn_s_down):
    f32 = np.float32
    x = np.asarray(L0_attn_x, dtype=f32)[0]
    xT = np.ascontiguousarray(x.T)
    w_order = (L0_attn_Wzg_w, L0_attn_Wz_w, L0_attn_Wf_w, L0_attn_Wga_w,
               L0_attn_Wgb_w)
    wT = [np.ascontiguousarray(np.asarray(w, f32).T) for w in w_order]
    woT = np.ascontiguousarray(np.asarray(L0_attn_Wout_w, f32).T)

    def tile4_hd(wq):
        a = np.asarray(wq).astype(np.uint8)
        aT = a.T
        return np.ascontiguousarray(
            aT.reshape(DC, 128, HC, 128).transpose(2, 0, 1, 3))

    def tile4_dh(wq):
        a = np.asarray(wq).astype(np.uint8)
        aT = a.T
        return np.ascontiguousarray(
            aT.reshape(HC, 128, DC, 128).transpose(0, 2, 1, 3))

    def cols(v, n):
        return np.ascontiguousarray(np.asarray(v, f32).reshape(n, 128).T)

    base = {
        "rmsw": cols(L0_attn_rms_w, DC), "wfb": cols(L0_attn_Wf_b, SC),
        "wob": cols(L0_attn_Wout_b, DC), "sgv": cols(L0_ffn_s_gate, HC),
        "suv": cols(L0_ffn_s_up, HC), "sdv": cols(L0_ffn_s_down, DC),
        "woT": woT, "wgT4": tile4_hd(L0_ffn_wq_gate),
        "wuT4": tile4_hd(L0_ffn_wq_up), "wdT4": tile4_dh(L0_ffn_wq_down),
    }
    for i in range(5):
        base[f"w{i}T"] = wT[i]

    in_maps = []
    for c in range(NCORES):
        m = dict(base)
        m["xT"] = np.ascontiguousarray(xT[:, c * TC:(c + 1) * TC])
        msk = np.zeros((128, NCORES), f32)
        msk[:, :c] = 1.0
        m["msk"] = msk
        in_maps.append(m)
    return in_maps


def kernel(**inputs):
    in_maps = _prep_inputs(**inputs)
    if "prog" not in _NC_CACHE:
        _NC_CACHE["prog"] = build_program()
    nc = _NC_CACHE["prog"]
    res = run_bass_kernel_spmd(nc, in_maps, core_ids=list(range(NCORES)))
    out = np.empty((1, T, D), dtype=np.float32)
    for c in range(NCORES):
        out[0, c * TC:(c + 1) * TC, :] = res.results[c]["outT"].T
    return out
